# version 02
# baseline (speedup 1.0000x reference)
import numpy as np
from contextlib import ExitStack

_SKIP = set()

import concourse.bass as bass
import concourse.mybir as mybir
from concourse.bass_utils import run_bass_kernel_spmd

F32 = mybir.dt.float32
BF16 = mybir.dt.bfloat16
AF = mybir.ActivationFunctionType
ALU = mybir.AluOpType
AX = mybir.AxisListType

D = 2048
DC = 16
T = 2048
NTT = 16
F = 1408
FC = 11
NEXP = 8
EPS = 1e-5
KD = 1.0 / 2048.0


class Sem:
    def __init__(self, handle, name):
        self.h = handle
        self.name = name
        self.count = 0


class Sched:
    ENGINES = ("pe", "dve", "act", "pool", "sp")

    def __init__(self):
        self.steps = {e: [] for e in self.ENGINES}

    def op(self, engine, fn, waits=(), inc=None, n=1):
        tok = None
        if inc is not None:
            inc.count += n
            tok = (inc, inc.count)
        wmax = {}
        for w in waits:
            if w is None:
                continue
            s, v = w
            if v > 0:
                wmax[s] = max(wmax.get(s, 0), v)
        self.steps[engine].append((tuple(wmax.items()), fn, tok, n))
        return tok

    def replay(self, engine, eng, strict_sem=None):
        cnt = 0
        last_desig = {}
        for waits, fn, tok, n in self.steps[engine]:
            for (s, v) in waits:
                eng.wait_ge(s.h, v)
            if strict_sem is not None:
                if cnt > 0:
                    eng.wait_ge(strict_sem, cnt)
                for h, v in last_desig.items():
                    eng.wait_ge(h, v)
            inst = fn(eng)
            if tok is not None:
                if inst is None:
                    inst = eng.nop()
                inst.then_inc(tok[0].h, n)
                if strict_sem is not None:
                    last_desig[tok[0].h] = tok[1]
            elif strict_sem is not None:
                if inst is None:
                    inst = eng.nop()
                inst.then_inc(strict_sem, 1)
                cnt += 1

    def barrier(self, bsems):
        toks = {}
        for e in self.ENGINES:
            toks[e] = self.op(e, lambda g: g.drain(), inc=bsems[e])
        for e in self.ENGINES:
            waits = [toks[o] for o in self.ENGINES if o != e]
            self.op(e, lambda g: g.nop(), waits=waits)


def build_program(debug=False, phases=99, strict=False, n_tiles=32):
    nc = bass.Bass("TRN2", target_bir_lowering=False)

    x_mine = nc.dram_tensor("x_mine", [T, D], F32, kind="ExternalInput").ap()
    x_part = nc.dram_tensor("x_part", [T, D], F32, kind="ExternalInput").ap()
    wq = nc.dram_tensor("wq", [D, D], F32, kind="ExternalInput").ap()
    wk = nc.dram_tensor("wk", [D, D], F32, kind="ExternalInput").ap()
    wv = nc.dram_tensor("wv", [D, D], F32, kind="ExternalInput").ap()
    wo = nc.dram_tensor("wo", [D, D], F32, kind="ExternalInput").ap()
    wr_t = nc.dram_tensor("wr_t", [D, NEXP], F32, kind="ExternalInput").ap()
    wsg_c = nc.dram_tensor("wsg_c", [22, 128, DC, 128], F32, kind="ExternalInput").ap()
    wsu_c = nc.dram_tensor("wsu_c", [22, 128, DC, 128], F32, kind="ExternalInput").ap()
    wg_c = nc.dram_tensor("wg_c", [NEXP, FC, 128, DC, 128], F32, kind="ExternalInput").ap()
    wu_c = nc.dram_tensor("wu_c", [NEXP, FC, 128, DC, 128], F32, kind="ExternalInput").ap()
    wsd_t = nc.dram_tensor("wsd_t", [2 * F, D], F32, kind="ExternalInput").ap()
    wd_t = nc.dram_tensor("wd_t", [NEXP, F, D], F32, kind="ExternalInput").ap()

    out = nc.dram_tensor("out", [T, D], F32, kind="ExternalOutput").ap()

    dbg = "ExternalOutput" if debug else "Internal"
    xn_dram = nc.dram_tensor("xn_dram", [2 * T, D], BF16, kind=dbg).ap()
    g_dram = nc.dram_tensor("g_dram", [DC, 128, D], BF16, kind=dbg).ap()
    v_dram = nc.dram_tensor("v_dram", [NTT, DC, 128, 128], BF16, kind=dbg).ap()
    gff_dram = nc.dram_tensor("gff_dram", [T, D], BF16, kind=dbg).ap()
    w_dram = nc.dram_tensor("w_dram", [128, NTT * NEXP], F32, kind=dbg).ap()
    r_dram = nc.dram_tensor("r_dram", [DC, 128, D], BF16, kind=dbg).ap()
    aw_dram = nc.dram_tensor("aw_dram", [DC, 128, D], BF16, kind=dbg).ap()
    m_dram = nc.dram_tensor("m_dram", [DC, 128, D], BF16, kind=dbg).ap()

    sched = Sched()

    base0 = nc.sbuf_base
    base = (base0 + 31) & ~31

    def place(name, shape, dtype, off):
        return nc.alloc_sbuf_tensor_at(name, shape, dtype, offset=base + off)

    O_RA = 0
    O_RB = 65536
    O_RC = 131072
    O_SPARE = 196608
    O_SM = 204800
    TOTAL = 206848
    nc.alloc_sbuf_tensor("reserve", [128, (base - base0) + TOTAL], mybir.dt.uint8)
    ra = place("ra", [128, 32768], BF16, O_RA)
    rb = place("rb", [128, 32768], BF16, O_RB)
    rb_f = rb.bitcast(F32)
    rc = place("rc", [128, 32768], BF16, O_RC)
    rc_f = rc.bitcast(F32)
    spare = place("spare", [128, 2048], F32, O_SPARE)
    wr_sb = place("wr_sb", [128, DC * NEXP], BF16, O_SM)
    w_sb = place("w_sb", [128, NTT * NEXP], F32, O_SM + 256)
    stats = place("stats", [128, 64], F32, O_SM + 768)
    rsm = place("rsm", [128, 64], F32, O_SM + 1024)
    ident = place("ident", [128, 128], F32, O_SM + 1280)

    with ExitStack() as stack:
        e_ = stack.enter_context
        psA = e_(nc.psum_tensor("psA", [128, 2048], F32))
        psB = e_(nc.psum_tensor("psB", [128, 2048], F32))

        sems = {}
        for name in ("q_sp", "q_gp", "s_pe", "s_dve", "s_act", "s_pool", "s_sp"):
            sems[name] = Sem(e_(nc.semaphore(name)), name)
        q_sp, q_gp = sems["q_sp"], sems["q_gp"]
        s_pe, s_dve, s_act = sems["s_pe"], sems["s_dve"], sems["s_act"]
        s_pool, s_sp = sems["s_pool"], sems["s_sp"]
        bsems = {"pe": s_pe, "dve": s_dve, "act": s_act, "pool": s_pool,
                 "sp": s_sp}

        bank = {(t, i): {"w": None, "r": None} for t in "AB" for i in range(4)}

        def ps(t, i, n=1):
            tens = psA if t == "A" else psB
            return tens[:, i * 512:(i + n) * 512]

        def xnf_s(dc_, t0, t1):
            return ra[:, dc_ * T + t0: dc_ * T + t1]

        wr_tok = sched.op(
            "pool",
            lambda g: g.dma_start(
                out=wr_sb[:, :],
                in_=wr_t.rearrange("(dc p) e -> p dc e", p=128)),
            inc=q_gp, n=16)
        io_tok = sched.op(
            "pool",
            lambda g: g.iota(spare[:, 0:128], [[-1, 128]], base=0,
                             channel_multiplier=1,
                             allow_small_or_imprecise_dtypes=True),
            inc=s_pool)
        ident_tok = sched.op(
            "dve",
            lambda g: g.tensor_single_scalar(
                ident[:, :], spare[:, 0:128], 0.0, ALU.is_equal),
            waits=[io_tok], inc=s_dve)

        if phases < 1:
            _finish(sched, bsems, q_sp, out, rb_f, debug, w_dram, w_sb)
            _replay(nc, sched, strict)
            return nc

        xstg = [rc_f[:, i * 2048:(i + 1) * 2048] for i in range(2)]
        xnf32 = [rc_f[:, 4096 + i * 2048: 4096 + (i + 1) * 2048]
                 for i in range(2)]
        pstg = [rc[:, 8192 + i * 2048: 8192 + (i + 1) * 2048] for i in range(2)]
        ST = {k: i for i, k in enumerate(
            ["s1", "sq", "m", "msq", "mm", "var", "lnv", "rstd"])}

        def st(slot, k):
            c = slot * 16 + ST[k]
            return stats[:, c:c + 1]

        xstg_free = [None, None]
        xnf32_free = [None, None]
        pstg_free = [None, None]
        xn_tok = [None] * 32
        xd_tok = [None] * 32
        xc_prev = None
        tr15 = None

        for i in range(n_tiles):
            part, tt = (0, i) if i < 16 else (1, i - 16)
            src = x_mine if part == 0 else x_part
            sl = i % 2
            lx = sched.op(
                "sp",
                lambda g, s=src, t=tt, sl=sl: g.dma_start(
                    out=xstg[sl], in_=s[t * 128:(t + 1) * 128, :]),
                waits=[xstg_free[sl]], inc=q_sp, n=16)
            if "stats" in _SKIP:
                import os as _os
                if part == 0:
                    if "noap" not in _SKIP:
                        t_ap = sched.op("dve", lambda g, sl=sl: g.tensor_scalar(
                            xnf32[sl], xstg[sl], 0.0, 1.0, ALU.subtract, ALU.mult),
                            waits=[lx, xnf32_free[sl]], inc=s_dve)
                    else:
                        t_ap = sched.op("sp", lambda g, sl=sl: g.dma_start(
                            out=xnf32[sl], in_=x_mine[tt * 128:(tt + 1) * 128, :]),
                            waits=[xnf32_free[sl]], inc=q_sp, n=16)
                    if "noxnt" not in _SKIP:
                        t_cp = sched.op("dve", lambda g, sl=sl, t=tt: g.tensor_copy(
                            rb[:, t * 2048:(t + 1) * 2048], xnf32[sl]),
                            waits=[t_ap], inc=s_dve)
                    else:
                        t_cp = t_ap
                    xn_tok[i] = t_cp
                    xstg_free[sl] = t_ap
                    if "st" not in _SKIP:
                        t_st = sched.op("sp", lambda g, t=tt: g.dma_start(
                            out=xn_dram[t * 128:(t + 1) * 128, :],
                            in_=rb[:, t * 2048:(t + 1) * 2048]),
                            waits=[t_cp], inc=q_sp, n=16)
                        xd_tok[i] = t_st
                    last_tr = None
                    import os as _os
                    _ntr = int(_os.environ.get("K_NTR", str(DC)))
                    for k in range(_ntr):
                        w = [t_ap, xc_prev] if k == 0 else []
                        if i == 0 and k == 0:
                            w.append(ident_tok)
                        last_tr = sched.op(
                            "pe",
                            lambda g, sl=sl, kk=k: g.transpose(
                                psA[:, kk * 128:(kk + 1) * 128],
                                xnf32[sl][:, kk * 128:(kk + 1) * 128],
                                ident[:, :]),
                            waits=w, inc=s_pe if k == _ntr - 1 else None)
                    tr15 = last_tr
                    xnf32_free[sl] = last_tr
                    xc_prev = sched.op(
                        "dve",
                        lambda g, t=tt: g.tensor_copy(
                            bass.AP(ra, t * 128, [[32768, 128], [T, DC], [1, 128]]),
                            psA[:, 0:2048]),
                        waits=[last_tr], inc=s_dve)
                p1_sq_tok = None
                continue
            sched.op(
                "dve",
                lambda g, sl=sl: g.reduce_sum(st(sl, "s1"), xstg[sl], axis=AX.X),
                waits=[lx], inc=s_dve)
            t_sq = sched.op(
                "act",
                lambda g, sl=sl: g.activation(
                    psB[:, 0:2048], xstg[sl], AF.Square,
                    accum_out=st(sl, "sq")),
                waits=[lx], inc=s_act)
            sched.op("dve", lambda g, sl=sl: g.tensor_scalar(
                st(sl, "m"), st(sl, "s1"), KD, None, ALU.mult))
            t_msq = sched.op("dve", lambda g, sl=sl: g.tensor_scalar(
                st(sl, "msq"), st(sl, "sq"), KD, None, ALU.mult),
                waits=[t_sq])
            sched.op("dve", lambda g, sl=sl: g.tensor_mul(
                st(sl, "mm"), st(sl, "m"), st(sl, "m")))
            t_var = sched.op("dve", lambda g, sl=sl: g.tensor_scalar(
                st(sl, "var"), st(sl, "msq"), st(sl, "mm"), EPS,
                ALU.subtract, ALU.add), inc=s_dve)
            t_ln = sched.op("act", lambda g, sl=sl: g.activation(
                st(sl, "lnv"), st(sl, "var"), AF.Ln),
                waits=[t_var], inc=s_act)
            t_rstd = sched.op("act", lambda g, sl=sl: g.activation(
                st(sl, "rstd"), st(sl, "lnv"), AF.Exp, scale=-0.5),
                inc=s_act)
            if part == 0:
                t_ap = sched.op("dve", lambda g, sl=sl: g.tensor_scalar(
                    xnf32[sl], xstg[sl], st(sl, "m"), st(sl, "rstd"),
                    ALU.subtract, ALU.mult),
                    waits=[t_rstd, xnf32_free[sl]], inc=s_dve)
                t_cp = sched.op("dve", lambda g, sl=sl, t=tt: g.tensor_copy(
                    rb[:, t * 2048:(t + 1) * 2048], xnf32[sl]), inc=s_dve)
                xn_tok[i] = t_cp
                xstg_free[sl] = t_ap
                if "st" not in _SKIP:
                    t_st = sched.op("sp", lambda g, t=tt: g.dma_start(
                        out=xn_dram[t * 128:(t + 1) * 128, :],
                        in_=rb[:, t * 2048:(t + 1) * 2048]),
                        waits=[t_cp], inc=q_sp, n=16)
                    xd_tok[i] = t_st
                last_tr = None
                import os as _os
                _ntr = int(_os.environ.get("K_NTR", str(DC)))
                for k in range(_ntr if ("tp" not in _SKIP and "cponly" not in _SKIP) else 0):
                    w = [t_ap, xc_prev] if k == 0 else []
                    if i == 0 and k == 0:
                        w.append(ident_tok)
                    last_tr = sched.op(
                        "pe",
                        lambda g, sl=sl, kk=k: g.transpose(
                            psA[:, kk * 128:(kk + 1) * 128],
                            xnf32[sl][:, kk * 128:(kk + 1) * 128],
                            ident[:, :]),
                        waits=w, inc=s_pe if k == DC - 1 else None)
                tr15 = last_tr
                xnf32_free[sl] = last_tr
                if "cponly" in _SKIP:
                    xc_prev = sched.op(
                        "dve",
                        lambda g, t=tt: g.tensor_copy(
                            ra[:, 0:2048], psA[:, 0:2048]),
                        waits=[t_ap], inc=s_dve)
                elif "tp" not in _SKIP and "noxc" not in _SKIP:
                    if "xc" in _SKIP:
                        xc_prev = sched.op(
                            "dve",
                            lambda g, t=tt: g.tensor_copy(
                                ra[:, 0:2048], psA[:, 0:2048]),
                            waits=[last_tr], inc=s_dve)
                    else:
                        xc_prev = sched.op(
                            "dve",
                            lambda g, t=tt: g.tensor_copy(
                                bass.AP(ra, t * 128, [[32768, 128], [T, DC], [1, 128]]),
                                psA[:, 0:2048]),
                            waits=[last_tr], inc=s_dve)
            else:
                w = [t_rstd, pstg_free[sl]]
                if i == 16:
                    w.append(tr15)
                t_ap = sched.op("dve", lambda g, sl=sl: g.tensor_scalar(
                    pstg[sl], xstg[sl], st(sl, "m"), st(sl, "rstd"),
                    ALU.subtract, ALU.mult), waits=w, inc=s_dve)
                xn_tok[i] = t_ap
                xstg_free[sl] = t_ap
                t_st = sched.op("sp", lambda g, sl=sl, t=tt: g.dma_start(
                    out=xn_dram[T + t * 128: T + (t + 1) * 128, :],
                    in_=pstg[sl]),
                    waits=[t_ap], inc=q_sp, n=16)
                xd_tok[i] = t_st
                pstg_free[sl] = t_st
            p1_sq_tok = t_sq

        tr_tok = xc_prev

        if phases < 2:
            _finish(sched, bsems, q_sp, out, rb_f, debug, w_dram, w_sb)
            _replay(nc, sched, strict)
            return nc

        pring = [rc[:, 12288 + i * 2048: 12288 + (i + 1) * 2048]
                 for i in range(4)]
        gstg = [rc[:, 20480 + i * 2048: 20480 + (i + 1) * 2048]
                for i in range(2)]
        pl_tok = {}
        cons_tok = {}
        gstg_free = [None, None]
        prev_evac = [None, None]

        def emit_pload(p, ptt, guard):
            pl_tok[(p, ptt)] = sched.op(
                "pool",
                lambda g, pt=ptt, sl=ptt % 4: g.dma_start(
                    out=pring[sl],
                    in_=xn_dram[T + pt * 128: T + (pt + 1) * 128, :]),
                waits=[xd_tok[16 + ptt], guard], inc=q_gp, n=16)

        for ptt in range(4):
            emit_pload(0, ptt, None)
        for p in range(8):
            for tile in range(32):
                if tile < 16:
                    src_t = rb[:, tile * 2048:(tile + 1) * 2048]
                    w = [xn_tok[tile]] if p == 0 else []
                else:
                    src_t = pring[(tile - 16) % 4]
                    w = [pl_tok[(p, tile - 16)]]
                if tile == 0:
                    w = w + [prev_evac[0], prev_evac[1]]
                    if p == 0:
                        w = w + [p1_sq_tok, tr_tok]
                last = None
                for ei, ec in enumerate((2 * p, 2 * p + 1)):
                    t = "A" if ei == 0 else "B"
                    for fb in range(4):
                        is_last = (ei == 1 and fb == 3)
                        last = sched.op(
                            "pe",
                            lambda g, s=src_t, e=ec, tt_=t, f=fb, ti=tile:
                                g.matmul(
                                    ps(tt_, f), s[:, e * 128:(e + 1) * 128],
                                    s[:, f * 512:(f + 1) * 512],
                                    start=(ti == 0), stop=(ti == 31)),
                            waits=w if (ei == 0 and fb == 0) else [],
                            inc=s_pe if (is_last and (tile >= 16 or tile == 31))
                                else None)
                if tile >= 16:
                    k = tile - 16
                    cons_tok[(p, k)] = last
                    if k + 4 < 16:
                        emit_pload(p, k + 4, last)
                    elif p + 1 < 8:
                        emit_pload(p + 1, k - 12, last)
            stop_tok = last
            for ei in range(2):
                t = "A" if ei == 0 else "B"
                w = [stop_tok, gstg_free[ei]]
                ev = sched.op(
                    "dve",
                    lambda g, tt_=t, sl=ei: g.tensor_copy(
                        gstg[sl], ps(tt_, 0, 4)),
                    waits=w, inc=s_dve)
                prev_evac[ei] = ev
                sc = 2 * p + ei
                stok = sched.op(
                    "sp",
                    lambda g, sl=ei, s=sc: g.dma_start(
                        out=g_dram[s], in_=gstg[sl]),
                    waits=[ev], inc=q_sp, n=16)
                gstg_free[ei] = stok

        for i in range(4):
            bank[("A", i)]["r"] = prev_evac[0]
            bank[("B", i)]["r"] = prev_evac[1]

        RS = {k: i for i, k in enumerate(
            ["ls", "e1", "mask", "l3", "sel", "w1"])}

        def rs(k):
            c = RS[k] * 8
            return rsm[:, c:c + 8]

        def rps(tt):
            t = "A" if tt % 8 < 4 else "B"
            b = tt % 4
            off = (tt // 8) * 8
            tens = psA if t == "A" else psB
            return (t, b), tens[:, b * 512 + off: b * 512 + off + 8]

        RSM1 = rsm[:, 48:49]
        RSM2 = rsm[:, 49:50]
        RSM3 = rsm[:, 50:51]
        RSM4 = rsm[:, 51:52]
        rt_done = None
        t_e1_prev = None
        r_mm_tok = [None] * NTT
        r_cp_tok = [None] * NTT
        for tt in range(NTT):
            key, slc = rps(tt)
            for dc_ in range(DC):
                w = []
                if dc_ == 0:
                    w = [bank[key]["r"]]
                    if tt == 0:
                        w += [tr_tok, wr_tok, xn_tok[15]]
                r_mm_tok[tt] = sched.op(
                    "pe",
                    lambda g, t=tt, d=dc_, s=slc: g.matmul(
                        s, xnf_s(d, t * 128, (t + 1) * 128),
                        wr_sb[:, d * 8:(d + 1) * 8],
                        start=(d == 0), stop=(d == DC - 1)),
                    waits=w, inc=s_pe if dc_ == DC - 1 else None)
            bank[key]["w"] = r_mm_tok[tt]
            r_cp_tok[tt] = sched.op(
                "dve", lambda g, s=slc: g.tensor_copy(rs("ls"), s),
                waits=[r_mm_tok[tt], t_e1_prev], inc=s_dve)
            bank[key]["r"] = r_cp_tok[tt]
            sched.op("dve", lambda g: g.reduce_max(
                RSM1, rs("ls"), axis=AX.X, negate=True),
                waits=[t_e1_prev], inc=s_dve)
            t_e1 = sched.op("act", lambda g: g.activation(
                rs("e1"), rs("ls"), AF.Exp, bias=RSM1, accum_out=RSM2),
                waits=[(s_dve, s_dve.count)], inc=s_act)
            t_e1_prev = t_e1
            sched.op("dve", lambda g: g.reciprocal(RSM3, RSM2),
                     waits=[t_e1])
            sched.op("dve", lambda g: g.tensor_scalar(
                rs("mask"), rs("ls"), RSM1, 0.0, ALU.add, ALU.is_ge))
            sched.op("dve", lambda g: g.scalar_tensor_tensor(
                rs("l3"), rs("mask"), -1e9, rs("ls"), ALU.mult, ALU.add))
            sched.op("dve", lambda g: g.reduce_max(
                RSM4, rs("l3"), axis=AX.X, negate=True))
            sched.op("dve", lambda g: g.tensor_scalar(
                rs("sel"), rs("ls"), RSM4, 0.0, ALU.add, ALU.is_ge))
            sched.op("dve", lambda g: g.tensor_scalar(
                rs("w1"), rs("e1"), RSM3, None, ALU.mult))
            rt_done = sched.op("dve", lambda g, t=tt: g.tensor_mul(
                w_sb[:, t * 8:(t + 1) * 8], rs("w1"), rs("sel")), inc=s_dve)

        wvb = [rc[:, i * 8192:(i + 1) * 8192] for i in range(2)]
        vstg = [rc[:, 16384 + i * 512: 16384 + (i + 1) * 512] for i in range(4)]
        wv_tok = [None] * 4
        wvb_free = [None, None]
        vstg_free = [None] * 4
        vs_idx = 0
        wv3 = wv.rearrange("(ec p) f -> p ec f", p=128)
        gram_done = stop_tok
        slot_i = 0
        for gidx in range(4):
            wv_tok[gidx] = sched.op(
                "pool",
                lambda g, gi=gidx: g.dma_start(
                    out=bass.AP(rc, (gi % 2) * 8192,
                                [[32768, 128], [512, DC], [1, 512]]),
                    in_=wv3[:, :, gi * 512:(gi + 1) * 512]),
                waits=[wvb_free[gidx % 2], gram_done], inc=q_gp, n=16)
            for j in range(4):
                fvc = gidx * 4 + j
                for tb in range(4):
                    sl = slot_i % 8
                    t, i = ("A", sl) if sl < 4 else ("B", sl - 4)
                    last = None
                    for ec in range(DC):
                        w = []
                        if ec == 0:
                            w = [wv_tok[gidx], bank[(t, i)]["r"]]
                            if gidx == 0 and j == 0 and tb == 0:
                                w.append(rt_done)
                        last = sched.op(
                            "pe",
                            lambda g, tt_=t, ii=i, e=ec, gi=gidx, jj=j, b=tb:
                                g.matmul(
                                    ps(tt_, ii),
                                    wvb[gi % 2][:, e * 512 + jj * 128:
                                                e * 512 + (jj + 1) * 128],
                                    xnf_s(e, b * 512, (b + 1) * 512),
                                    start=(e == 0), stop=(e == DC - 1)),
                            waits=w, inc=s_pe if ec == DC - 1 else None)
                    bank[(t, i)]["w"] = last
                    if fvc == gidx * 4 + 3 and tb == 3:
                        wvb_free[gidx % 2] = last
                    vsl = vs_idx % 4
                    vs_idx += 1
                    ev = sched.op(
                        "dve",
                        lambda g, tt_=t, ii=i, vs=vsl: g.tensor_copy(
                            vstg[vs], ps(tt_, ii)),
                        waits=[last, vstg_free[vsl]], inc=s_dve)
                    bank[(t, i)]["r"] = ev
                    stok = sched.op(
                        "sp",
                        lambda g, vs=vsl, fv=fvc, b=tb: g.dma_start(
                            out=v_dram[b * 4:(b + 1) * 4, fv, :, :].rearrange(
                                "a p t -> p a t"),
                            in_=bass.AP(
                                rc, 16384 + vs * 512,
                                [[32768, 128], [128, 4], [1, 128]])),
                        waits=[ev], inc=q_sp, n=16)
                    vstg_free[vsl] = stok
                    slot_i += 1

        sched.barrier(bsems)
        if phases < 3:
            _finish(sched, bsems, q_sp, out, rb_f, debug, w_dram, w_sb)
            _replay(nc, sched, strict)
            return nc

        HF0 = 0
        WD0 = 11264
        WG0 = 22528
        WU0 = 26624
        GFF0 = 30720

        def hf(fc_, t0, t1):
            return rc[:, HF0 + fc_ * 1024 + t0: HF0 + fc_ * 1024 + t1]

        wd_ring = [rc[:, WD0 + i * 5632: WD0 + (i + 1) * 5632] for i in range(2)]
        wg_ring = [rc[:, WG0 + i * 2048: WG0 + (i + 1) * 2048] for i in range(2)]
        wu_ring = [rc[:, WU0 + i * 2048: WU0 + (i + 1) * 2048] for i in range(2)]
        gffs = [rc[:, GFF0:GFF0 + 2048], rc[:, GFF0:GFF0 + 2048]]
        gs = [spare[:, i * 512:(i + 1) * 512] for i in range(4)]

        experts = [("sh", 0), ("sh", 1)] + [("ex", e) for e in range(NEXP)]
        pairs = [(0, 1), (2, 3), (4, 5), (6, 7), (8, 9), (10,)]

        gs_free = [None] * 4
        gffs_free = [None, None]
        out_guard = [None] * 16
        wg_cons = {}
        wu_cons = {}
        wd_cons = {}
        wgi = 0
        wdi = 0
        gff_store = [None] * 16

        for th in range(2):
            for exi, (kind, eidx) in enumerate(experts):
                h_toks = []
                for pi, pr in enumerate(pairs):
                    gtoks = {}
                    utoks = {}
                    for fc_ in pr:
                        gi = wgi + fc_
                        sl = gi % 2
                        if kind == "sh":
                            srcg = wsg_c[eidx * FC + fc_]
                            srcu = wsu_c[eidx * FC + fc_]
                        else:
                            srcg = wg_c[eidx, fc_]
                            srcu = wu_c[eidx, fc_]
                        gtoks[fc_] = sched.op(
                            "pool",
                            lambda g, s=srcg, sl=sl: g.dma_start(
                                out=wg_ring[sl],
                                in_=s.rearrange("p dc ff -> p (dc ff)")),
                            waits=[wg_cons.get(gi - 2)], inc=q_gp, n=16)
                        utoks[fc_] = sched.op(
                            "pool",
                            lambda g, s=srcu, sl=sl: g.dma_start(
                                out=wu_ring[sl],
                                in_=s.rearrange("p dc ff -> p (dc ff)")),
                            waits=[wu_cons.get(gi - 2)], inc=q_gp, n=16)
                    gstop = {}
                    for si, fc_ in enumerate(pr):
                        wsl = (wgi + fc_) % 2
                        for tb in range(2):
                            slot = si * 2 + tb
                            last = None
                            for dc_ in range(DC):
                                w = []
                                if dc_ == 0:
                                    w = [gtoks[fc_], bank[("A", slot)]["r"]]
                                last = sched.op(
                                    "pe",
                                    lambda g, sl=wsl, d=dc_, b=tb,
                                           so=slot, t=th:
                                        g.matmul(
                                            ps("A", so),
                                            wg_ring[sl][:, d * 128:(d + 1) * 128],
                                            xnf_s(d, t * 1024 + b * 512,
                                                  t * 1024 + (b + 1) * 512),
                                            start=(d == 0), stop=(d == DC - 1)),
                                    waits=w, inc=s_pe if dc_ == DC - 1 else None)
                            bank[("A", slot)]["w"] = last
                            gstop[slot] = last
                        wg_cons[wgi + fc_] = last
                    ustop = {}
                    for si, fc_ in enumerate(pr):
                        wsl = (wgi + fc_) % 2
                        for tb in range(2):
                            slot = si * 2 + tb
                            last = None
                            for dc_ in range(DC):
                                w = []
                                if dc_ == 0:
                                    w = [utoks[fc_], bank[("B", slot)]["r"]]
                                last = sched.op(
                                    "pe",
                                    lambda g, sl=wsl, d=dc_, b=tb,
                                           so=slot, t=th:
                                        g.matmul(
                                            ps("B", so),
                                            wu_ring[sl][:, d * 128:(d + 1) * 128],
                                            xnf_s(d, t * 1024 + b * 512,
                                                  t * 1024 + (b + 1) * 512),
                                            start=(d == 0), stop=(d == DC - 1)),
                                    waits=w, inc=s_pe if dc_ == DC - 1 else None)
                            bank[("B", slot)]["w"] = last
                            ustop[slot] = last
                        wu_cons[wgi + fc_] = last
                    for si, fc_ in enumerate(pr):
                        for tb in range(2):
                            slot = si * 2 + tb
                            t_sil = sched.op(
                                "act",
                                lambda g, so=slot: g.activation(
                                    gs[so], ps("A", so), AF.Silu),
                                waits=[gstop[slot], gs_free[slot]],
                                inc=s_act)
                            bank[("A", slot)]["r"] = t_sil
                            t_h = sched.op(
                                "dve",
                                lambda g, so=slot, f=fc_, b=tb: g.tensor_tensor(
                                    hf(f, b * 512, (b + 1) * 512),
                                    gs[so], ps("B", so), ALU.mult),
                                waits=[ustop[slot], t_sil], inc=s_dve)
                            bank[("B", slot)]["r"] = t_h
                            gs_free[slot] = t_h
                            h_toks.append(t_h)
                wgi += FC

                h_all = h_toks[-1]
                for db in range(4):
                    gi = wdi + db
                    wsl = gi % 2
                    if kind == "sh":
                        srcd = wsd_t[eidx * F:(eidx + 1) * F,
                                     db * 512:(db + 1) * 512]
                    else:
                        srcd = wd_t[eidx][:, db * 512:(db + 1) * 512]
                    wd_tok = sched.op(
                        "pool",
                        lambda g, s=srcd, sl=wsl: g.dma_start(
                            out=wd_ring[sl],
                            in_=s.rearrange("(fc p) d -> p fc d", p=128)),
                        waits=[wd_cons.get(gi - 2)], inc=q_gp, n=16)
                    ev_toks = []
                    for tt in range(8):
                        t, i = ("A", tt) if tt < 4 else ("B", tt - 4)
                        last = None
                        for fc_ in range(FC):
                            w = []
                            if fc_ == 0:
                                w = [wd_tok, bank[(t, i)]["r"], h_all]
                            last = sched.op(
                                "pe",
                                lambda g, sl=wsl, f=fc_, tt_=t, ii=i, tl=tt:
                                    g.matmul(
                                        ps(tt_, ii),
                                        hf(f, tl * 128, (tl + 1) * 128),
                                        wd_ring[sl][:, f * 512:(f + 1) * 512],
                                        start=(f == 0), stop=(f == FC - 1)),
                                waits=w, inc=s_pe if fc_ == FC - 1 else None)
                        bank[(t, i)]["w"] = last
                        if tt == 7:
                            wd_cons[gi] = last
                        gtt = th * 8 + tt
                        oslice = rb_f[:, tt * 2048 + db * 512:
                                      tt * 2048 + (db + 1) * 512]
                        w = [last, out_guard[gtt]]
                        if exi == 0:
                            ev = sched.op(
                                "dve",
                                lambda g, o=oslice, tt_=t, ii=i: g.tensor_copy(
                                    o, ps(tt_, ii)),
                                waits=w, inc=s_dve)
                        elif kind == "sh":
                            ev = sched.op(
                                "dve",
                                lambda g, o=oslice, tt_=t, ii=i: g.tensor_add(
                                    o, o, ps(tt_, ii)),
                                waits=w, inc=s_dve)
                        else:
                            wcol = w_sb[:, gtt * 8 + eidx: gtt * 8 + eidx + 1]
                            ev = sched.op(
                                "dve",
                                lambda g, o=oslice, tt_=t, ii=i, wc=wcol:
                                    g.scalar_tensor_tensor(
                                        o, ps(tt_, ii), wc, o,
                                        ALU.mult, ALU.add),
                                waits=w, inc=s_dve)
                        bank[(t, i)]["r"] = ev
                        ev_toks.append(ev)
                wdi += 4
                if exi == len(experts) - 1:
                    for tt in range(8):
                        gtt = th * 8 + tt
                        sl = 0
                        t_g = sched.op(
                            "act",
                            lambda g, tl=tt, sl=sl: g.activation(
                                gffs[sl], rb_f[:, tl * 2048:(tl + 1) * 2048],
                                AF.Silu),
                            waits=[ev_toks[-1], gffs_free[sl]],
                            inc=s_act)
                        stok = sched.op(
                            "sp",
                            lambda g, sl=sl, gt=gtt: g.dma_start(
                                out=gff_dram[gt * 128:(gt + 1) * 128, :],
                                in_=gffs[sl]),
                            waits=[t_g], inc=q_sp, n=16)
                        gffs_free[sl] = stok
                        gff_store[gtt] = stok
                        out_guard[gtt] = t_g

        sched.barrier(bsems)
        if phases < 4:
            _finish(sched, bsems, q_sp, out, rb_f, debug, w_dram, w_sb)
            _replay(nc, sched, strict)
            return nc

        g_tok = [None] * DC
        wk_tok = [None] * DC
        for sc in range(DC):
            g_tok[sc] = sched.op(
                "pool", lambda g, s=sc: g.dma_start(
                    out=ra[:, s * 2048:(s + 1) * 2048], in_=g_dram[s]),
                inc=q_gp, n=16)
            wk_tok[sc] = sched.op(
                "pool", lambda g, s=sc: g.dma_start(
                    out=rb[:, s * 2048:(s + 1) * 2048],
                    in_=wk[s * 128:(s + 1) * 128, :]),
                inc=q_gp, n=16)

        def slab_pass(lhs_slab, rhs_slab, dst_slab, n_out, load_toks,
                      extra_first=()):
            evs = []
            for oc in range(n_out):
                t = "A" if oc % 2 == 0 else "B"
                last = None
                for k in range(DC):
                    for cb in range(4):
                        w = []
                        if k == 0 and cb == 0:
                            w = list(extra_first) + [bank[(t, 0)]["r"],
                                                     bank[(t, 1)]["r"],
                                                     bank[(t, 2)]["r"],
                                                     bank[(t, 3)]["r"]]
                            w += load_toks
                        last = sched.op(
                            "pe",
                            lambda g, tt_=t, kk=k, c=cb, o=oc: g.matmul(
                                ps(tt_, c), lhs_slab(kk, o), rhs_slab(kk, c),
                                start=(kk == 0), stop=(kk == DC - 1)),
                            waits=w,
                            inc=s_pe if (k == DC - 1 and cb == 3) else None)
                for i in range(4):
                    bank[(t, i)]["w"] = last
                ev = sched.op(
                    "dve",
                    lambda g, tt_=t, o=oc: g.tensor_copy(
                        dst_slab(o), ps(tt_, 0, 4)),
                    waits=[last], inc=s_dve)
                for i in range(4):
                    bank[(t, i)]["r"] = ev
                evs.append(ev)
            return evs

        r_evs = slab_pass(
            lhs_slab=lambda k, o: ra[:, k * 2048 + o * 128: k * 2048 + (o + 1) * 128],
            rhs_slab=lambda k, c: rb[:, k * 2048 + c * 512: k * 2048 + (c + 1) * 512],
            dst_slab=lambda o: rc[:, o * 2048:(o + 1) * 2048],
            n_out=DC, load_toks=[g_tok[-1], wk_tok[-1]])

        sched.barrier(bsems)
        if phases < 5:
            if debug:
                for o in range(DC):
                    sched.op("sp", lambda g, oo=o: g.dma_start(
                        out=r_dram[oo], in_=rc[:, oo * 2048:(oo + 1) * 2048]),
                        waits=[r_evs[o]], inc=q_sp, n=16)
            _finish(sched, bsems, q_sp, out, rb_f, debug, w_dram, w_sb)
            _replay(nc, sched, strict)
            return nc

        wq_tok = [None] * DC
        for sc in range(DC):
            wq_tok[sc] = sched.op(
                "pool", lambda g, s=sc: g.dma_start(
                    out=ra[:, s * 2048:(s + 1) * 2048],
                    in_=wq[s * 128:(s + 1) * 128, :]),
                inc=q_gp, n=16)

        NM = rsm[:, 52:53]
        NM8 = rsm[:, 53:54]
        SSUM = rsm[:, 54:55]
        RSUM = rsm[:, 55:56]
        w_evs = []
        for ac in range(DC):
            t = "A" if ac % 2 == 0 else "B"
            last = None
            for k in range(DC):
                for cb in range(4):
                    w = []
                    if k == 0 and cb == 0:
                        w = [wq_tok[-1], bank[(t, 0)]["r"], bank[(t, 1)]["r"],
                             bank[(t, 2)]["r"], bank[(t, 3)]["r"]]
                    last = sched.op(
                        "pe",
                        lambda g, tt_=t, kk=k, c=cb, o=ac: g.matmul(
                            ps(tt_, c),
                            ra[:, kk * 2048 + o * 128: kk * 2048 + (o + 1) * 128],
                            rc[:, kk * 2048 + c * 512: kk * 2048 + (c + 1) * 512],
                            start=(kk == 0), stop=(kk == DC - 1)),
                        waits=w,
                        inc=s_pe if (k == DC - 1 and cb == 3) else None)
            for i in range(4):
                bank[(t, i)]["w"] = last
            t_nm = sched.op("dve", lambda g, tt_=t: g.reduce_max(
                NM, ps(tt_, 0, 4), axis=AX.X, negate=True),
                waits=[last, w_evs[-1] if w_evs else None], inc=s_dve)
            sched.op("dve", lambda g: g.tensor_scalar(
                NM8, NM, 0.125, None, ALU.mult), inc=s_dve)
            t_exp = sched.op("act", lambda g, tt_=t: g.activation(
                spare[:, 0:2048], ps(tt_, 0, 4), AF.Exp,
                scale=0.125, bias=NM8, accum_out=SSUM),
                waits=[(s_dve, s_dve.count)], inc=s_act)
            for i in range(4):
                bank[(t, i)]["r"] = t_exp
            t_rs = sched.op("dve", lambda g: g.reciprocal(RSUM, SSUM),
                            waits=[t_exp], inc=s_dve)
            ev = sched.op("dve", lambda g, o=ac: g.tensor_scalar(
                rb[:, o * 2048:(o + 1) * 2048], spare[:, 0:2048],
                RSUM, None, ALU.mult), waits=[t_rs], inc=s_dve)
            w_evs.append(ev)

        sched.barrier(bsems)
        if phases < 6:
            if debug:
                for o in range(DC):
                    sched.op("sp", lambda g, oo=o: g.dma_start(
                        out=aw_dram[oo], in_=rb[:, oo * 2048:(oo + 1) * 2048]),
                        waits=[w_evs[o]], inc=q_sp, n=16)
            _finish(sched, bsems, q_sp, out, rb_f, debug, w_dram, w_sb)
            _replay(nc, sched, strict)
            return nc

        wo_tok = [None] * DC
        for sc in range(DC):
            wo_tok[sc] = sched.op(
                "pool", lambda g, s=sc: g.dma_start(
                    out=rc[:, s * 2048:(s + 1) * 2048],
                    in_=wo[s * 128:(s + 1) * 128, :]),
                inc=q_gp, n=16)

        m_evs = slab_pass(
            lhs_slab=lambda k, o: rb[:, k * 2048 + o * 128: k * 2048 + (o + 1) * 128],
            rhs_slab=lambda k, c: rc[:, k * 2048 + c * 512: k * 2048 + (c + 1) * 512],
            dst_slab=lambda o: ra[:, o * 2048:(o + 1) * 2048],
            n_out=DC, load_toks=[wo_tok[-1]])

        sched.barrier(bsems)

        vring = [rc[:, i * 2048:(i + 1) * 2048] for i in range(2)]
        gfr = [rc[:, 4096 + i * 2048: 4096 + (i + 1) * 2048] for i in range(2)]
        xr = [rb_f[:, i * 2048:(i + 1) * 2048] for i in range(2)]
        fstg = [rb_f[:, 4096 + i * 2048: 4096 + (i + 1) * 2048]
                for i in range(2)]
        vring_free = [None, None]
        gfr_free = [None, None]
        xr_free = [None, None]
        fstg_free = [None, None]
        out_toks = []
        for tt in range(NTT):
            sl = tt % 2
            vt = sched.op(
                "pool", lambda g, t=tt, sl=sl: g.dma_start(
                    out=vring[sl],
                    in_=v_dram[t].rearrange("fc p t -> p fc t")),
                waits=[vring_free[sl]], inc=q_gp, n=16)
            gt = sched.op(
                "sp", lambda g, t=tt, sl=sl: g.dma_start(
                    out=gfr[sl], in_=gff_dram[t * 128:(t + 1) * 128, :]),
                waits=[gfr_free[sl]], inc=q_sp, n=16)
            xt = sched.op(
                "sp", lambda g, t=tt, sl=sl: g.dma_start(
                    out=xr[sl], in_=x_mine[t * 128:(t + 1) * 128, :]),
                waits=[xr_free[sl]], inc=q_sp, n=16)
            t_ = "A" if sl == 0 else "B"
            last = None
            for fc_ in range(DC):
                for gb in range(4):
                    w = []
                    if fc_ == 0 and gb == 0:
                        w = [vt, bank[(t_, 0)]["r"], bank[(t_, 1)]["r"],
                             bank[(t_, 2)]["r"], bank[(t_, 3)]["r"]]
                    last = sched.op(
                        "pe",
                        lambda g, tt_=t_, k=fc_, c=gb, sl=sl: g.matmul(
                            ps(tt_, c),
                            vring[sl][:, k * 128:(k + 1) * 128],
                            ra[:, k * 2048 + c * 512: k * 2048 + (c + 1) * 512],
                            start=(k == 0), stop=(k == DC - 1)),
                        waits=w,
                        inc=s_pe if (fc_ == DC - 1 and gb == 3) else None)
            for i in range(4):
                bank[(t_, i)]["w"] = last
            vring_free[sl] = last
            t1 = sched.op(
                "dve", lambda g, tt_=t_, sl=sl: g.tensor_tensor(
                    fstg[sl], ps(tt_, 0, 4), gfr[sl], ALU.mult),
                waits=[last, gt, fstg_free[sl]], inc=s_dve)
            for i in range(4):
                bank[(t_, i)]["r"] = t1
            gfr_free[sl] = t1
            t2 = sched.op(
                "dve", lambda g, sl=sl: g.tensor_add(
                    fstg[sl], fstg[sl], xr[sl]),
                waits=[xt], inc=s_dve)
            xr_free[sl] = t2
            ot = sched.op(
                "sp", lambda g, t=tt, sl=sl: g.dma_start(
                    out=out[t * 128:(t + 1) * 128, :], in_=fstg[sl]),
                waits=[t2], inc=q_sp, n=16)
            fstg_free[sl] = ot
            out_toks.append(ot)

        if debug:
            sched.op("sp", lambda g: g.dma_start(out=w_dram, in_=w_sb[:, :]),
                     waits=[rt_done], inc=q_sp, n=16)
            for o in range(DC):
                sched.op("sp", lambda g, oo=o: g.dma_start(
                    out=m_dram[oo], in_=ra[:, oo * 2048:(oo + 1) * 2048]),
                    waits=[m_evs[o]], inc=q_sp, n=16)

        sched.op("pool", lambda g: g.nop(), waits=[(q_sp, q_sp.count)])
        _replay(nc, sched, strict)

    return nc


def _finish(sched, bsems, q_sp, out, rb_f, debug, w_dram, w_sb):
    import os
    if not os.environ.get("K_NOBAR"):
        sched.barrier(bsems)
    for tt in range(NTT):
        tok = sched.op(
            "sp", lambda g, t=tt: g.dma_start(
                out=out[t * 128:(t + 1) * 128, :], in_=rb_f[:, 0:2048]),
            inc=q_sp, n=16)
    if debug:
        sched.op("sp", lambda g: g.dma_start(out=w_dram, in_=w_sb[:, :]),
                 inc=q_sp, n=16)
    sched.op("pool", lambda g: g.nop(), waits=[(q_sp, q_sp.count)])


def _replay(nc, sched, strict=False):
    import os
    ss = {}
    dev_chain = () if os.environ.get("K_NOCHAIN") else ("dve", "act")
    for e in (Sched.ENGINES if strict else dev_chain):
        ss[e] = nc.alloc_semaphore("strict_" + e)
    with nc.Block() as block:
        @block.tensor
        def _(eng):
            sched.replay("pe", eng, ss.get("pe"))

        @block.vector
        def _(eng):
            sched.replay("dve", eng, ss.get("dve"))

        @block.scalar
        def _(eng):
            sched.replay("act", eng, ss.get("act"))

        @block.gpsimd
        def _(eng):
            sched.replay("pool", eng, ss.get("pool"))

        @block.sync
        def _(eng):
            sched.replay("sp", eng, ss.get("sp"))


def prep_inputs(L0_attn_x, L0_attn_Wq, L0_attn_Wk, L0_attn_Wv, L0_attn_Wo,
                L0_ffn_Wr, L0_ffn_Wg, L0_ffn_Wu, L0_ffn_Wd, L0_ffn_Wsg,
                L0_ffn_Wsu, L0_ffn_Wsd):
    asc = np.ascontiguousarray
    x = np.asarray(L0_attn_x, dtype=np.float32).reshape(4, 4096, D)
    common = {
        "wq": asc(np.asarray(L0_attn_Wq, np.float32)),
        "wk": asc(np.asarray(L0_attn_Wk, np.float32)),
        "wv": asc(np.asarray(L0_attn_Wv, np.float32)),
        "wo": asc(np.asarray(L0_attn_Wo, np.float32)),
        "wr_t": asc(np.asarray(L0_ffn_Wr, np.float32).T),
        "wsg_c": asc(np.asarray(L0_ffn_Wsg, np.float32)
                     .reshape(22, 128, DC, 128).transpose(0, 3, 2, 1)),
        "wsu_c": asc(np.asarray(L0_ffn_Wsu, np.float32)
                     .reshape(22, 128, DC, 128).transpose(0, 3, 2, 1)),
        "wg_c": asc(np.asarray(L0_ffn_Wg, np.float32)
                    .reshape(NEXP, FC, 128, DC, 128).transpose(0, 1, 4, 3, 2)),
        "wu_c": asc(np.asarray(L0_ffn_Wu, np.float32)
                    .reshape(NEXP, FC, 128, DC, 128).transpose(0, 1, 4, 3, 2)),
        "wsd_t": asc(np.asarray(L0_ffn_Wsd, np.float32).T),
        "wd_t": asc(np.asarray(L0_ffn_Wd, np.float32).transpose(0, 2, 1)),
    }
    in_maps = []
    for c in range(8):
        b, h = c // 2, c % 2
        m = dict(common)
        m["x_mine"] = asc(x[b, h * T:(h + 1) * T])
        m["x_part"] = asc(x[b, (1 - h) * T: (2 - h) * T])
        in_maps.append(m)
    return in_maps


def kernel(**inputs):
    in_maps = prep_inputs(**inputs)
    nc = build_program(debug=False)
    res = run_bass_kernel_spmd(nc, in_maps, core_ids=list(range(8)))
    shards = [res.results[c]["out"] for c in range(8)]
    full = np.stack(shards, axis=0).reshape(4, 4096, D).astype(np.float32)
    return full


# version 04
# speedup vs baseline: 1.1606x; 1.1606x over previous
import numpy as np
from contextlib import ExitStack

_SKIP = set()

import concourse.bass as bass
import concourse.mybir as mybir
from concourse.bass_utils import run_bass_kernel_spmd

F32 = mybir.dt.float32
BF16 = mybir.dt.bfloat16
AF = mybir.ActivationFunctionType
ALU = mybir.AluOpType
AX = mybir.AxisListType

D = 2048
DC = 16
T = 2048
NTT = 16
F = 1408
FC = 11
NEXP = 8
EPS = 1e-5
KD = 1.0 / 2048.0


class Sem:
    def __init__(self, handle, name):
        self.h = handle
        self.name = name
        self.count = 0


class Sched:
    ENGINES = ("pe", "dve", "act", "pool", "sp")

    def __init__(self):
        self.steps = {e: [] for e in self.ENGINES}

    def op(self, engine, fn, waits=(), inc=None, n=1):
        tok = None
        if inc is not None:
            inc.count += n
            tok = (inc, inc.count)
        wmax = {}
        for w in waits:
            if w is None:
                continue
            s, v = w
            if v > 0:
                wmax[s] = max(wmax.get(s, 0), v)
        self.steps[engine].append((tuple(wmax.items()), fn, tok, n))
        return tok

    def replay(self, engine, eng, strict_sem=None):
        cnt = 0
        last_desig = {}
        for waits, fn, tok, n in self.steps[engine]:
            for (s, v) in waits:
                eng.wait_ge(s.h, v)
            if strict_sem is not None:
                if cnt > 0:
                    eng.wait_ge(strict_sem, cnt)
                for h, v in last_desig.items():
                    eng.wait_ge(h, v)
            inst = fn(eng)
            if tok is not None:
                if inst is None:
                    inst = eng.nop()
                inst.then_inc(tok[0].h, n)
                if strict_sem is not None:
                    last_desig[tok[0].h] = tok[1]
            elif strict_sem is not None:
                if inst is None:
                    inst = eng.nop()
                inst.then_inc(strict_sem, 1)
                cnt += 1

    def barrier(self, bsems):
        toks = {}
        for e in self.ENGINES:
            toks[e] = self.op(e, lambda g: g.drain(), inc=bsems[e])
        for e in self.ENGINES:
            waits = [toks[o] for o in self.ENGINES if o != e]
            self.op(e, lambda g: g.nop(), waits=waits)


def build_program(debug=False, phases=99, strict=False, n_tiles=32):
    nc = bass.Bass("TRN2", target_bir_lowering=False)

    x_mine = nc.dram_tensor("x_mine", [T, D], F32, kind="ExternalInput").ap()
    x_part = nc.dram_tensor("x_part", [T, D], F32, kind="ExternalInput").ap()
    wq = nc.dram_tensor("wq", [D, D], F32, kind="ExternalInput").ap()
    wk = nc.dram_tensor("wk", [D, D], F32, kind="ExternalInput").ap()
    wv = nc.dram_tensor("wv", [D, D], F32, kind="ExternalInput").ap()
    wo = nc.dram_tensor("wo", [D, D], F32, kind="ExternalInput").ap()
    wr_t = nc.dram_tensor("wr_t", [D, NEXP], F32, kind="ExternalInput").ap()
    wsg_c = nc.dram_tensor("wsg_c", [22, 128, DC, 128], F32, kind="ExternalInput").ap()
    wsu_c = nc.dram_tensor("wsu_c", [22, 128, DC, 128], F32, kind="ExternalInput").ap()
    wg_c = nc.dram_tensor("wg_c", [NEXP, FC, 128, DC, 128], F32, kind="ExternalInput").ap()
    wu_c = nc.dram_tensor("wu_c", [NEXP, FC, 128, DC, 128], F32, kind="ExternalInput").ap()
    wsd_t = nc.dram_tensor("wsd_t", [2 * F, D], F32, kind="ExternalInput").ap()
    wd_t = nc.dram_tensor("wd_t", [NEXP, F, D], F32, kind="ExternalInput").ap()

    out = nc.dram_tensor("out", [T, D], F32, kind="ExternalOutput").ap()

    wkb = nc.dram_tensor("wkb", [D, D], BF16, kind="Internal").ap()
    wqb = nc.dram_tensor("wqb", [D, D], BF16, kind="Internal").ap()
    wob = nc.dram_tensor("wob", [D, D], BF16, kind="Internal").ap()
    wsgb = nc.dram_tensor("wsgb", [22, 128, 2048], BF16, kind="Internal").ap()
    wsub = nc.dram_tensor("wsub", [22, 128, 2048], BF16, kind="Internal").ap()
    wgb = nc.dram_tensor("wgb", [NEXP, FC, 128, 2048], BF16, kind="Internal").ap()
    wub = nc.dram_tensor("wub", [NEXP, FC, 128, 2048], BF16, kind="Internal").ap()
    wsdb = nc.dram_tensor("wsdb", [2, 4, 128, 5632], BF16, kind="Internal").ap()
    wdb = nc.dram_tensor("wdb", [NEXP, 4, 128, 5632], BF16, kind="Internal").ap()

    dbg = "ExternalOutput" if debug else "Internal"
    xn_dram = nc.dram_tensor("xn_dram", [2 * T, D], BF16, kind=dbg).ap()
    g_dram = nc.dram_tensor("g_dram", [DC, 128, D], BF16, kind=dbg).ap()
    v_dram = nc.dram_tensor("v_dram", [DC, 128, T], BF16, kind=dbg).ap()
    gff_dram = nc.dram_tensor("gff_dram", [T, D], BF16, kind=dbg).ap()
    w_dram = nc.dram_tensor("w_dram", [128, NTT * NEXP], F32, kind=dbg).ap()
    r_dram = nc.dram_tensor("r_dram", [DC, 128, D], BF16, kind=dbg).ap()
    aw_dram = nc.dram_tensor("aw_dram", [DC, 128, D], BF16, kind=dbg).ap()
    m_dram = nc.dram_tensor("m_dram", [DC, 128, D], BF16, kind=dbg).ap()

    sched = Sched()

    base0 = nc.sbuf_base
    base = (base0 + 31) & ~31

    def place(name, shape, dtype, off):
        return nc.alloc_sbuf_tensor_at(name, shape, dtype, offset=base + off)

    O_RA = 0
    O_RB = 65536
    O_RC = 131072
    O_SPARE = 196608
    O_SM = 204800
    TOTAL = 206848
    nc.alloc_sbuf_tensor("reserve", [128, (base - base0) + TOTAL], mybir.dt.uint8)
    ra = place("ra", [128, 32768], BF16, O_RA)
    rb = place("rb", [128, 32768], BF16, O_RB)
    rb_f = rb.bitcast(F32)
    rc = place("rc", [128, 32768], BF16, O_RC)
    rc_f = rc.bitcast(F32)
    spare = place("spare", [128, 2048], F32, O_SPARE)
    wr_sb = place("wr_sb", [128, DC * NEXP], BF16, O_SM)
    w_sb = place("w_sb", [128, NTT * NEXP], F32, O_SM + 256)
    stats = place("stats", [128, 64], F32, O_SM + 768)
    rsm = place("rsm", [128, 64], F32, O_SM + 1024)
    ident = place("ident", [128, 128], F32, O_SM + 1280)

    with ExitStack() as stack:
        e_ = stack.enter_context
        psA = e_(nc.psum_tensor("psA", [128, 2048], F32))
        psB = e_(nc.psum_tensor("psB", [128, 2048], F32))

        sems = {}
        for name in ("q_sp", "q_gp", "q_act", "s_pe", "s_dve", "s_act",
                     "s_pool", "s_sp"):
            sems[name] = Sem(e_(nc.semaphore(name)), name)
        q_sp, q_gp, q_act = sems["q_sp"], sems["q_gp"], sems["q_act"]
        s_pe, s_dve, s_act = sems["s_pe"], sems["s_dve"], sems["s_act"]
        s_pool, s_sp = sems["s_pool"], sems["s_sp"]
        bsems = {"pe": s_pe, "dve": s_dve, "act": s_act, "pool": s_pool,
                 "sp": s_sp}

        bank = {(t, i): {"w": None, "r": None} for t in "AB" for i in range(4)}

        def ps(t, i, n=1):
            tens = psA if t == "A" else psB
            return tens[:, i * 512:(i + n) * 512]

        def xnf_s(dc_, t0, t1):
            return ra[:, dc_ * T + t0: dc_ * T + t1]

        def hw_dma(i, fn, waits=()):
            if i % 2 == 0:
                return sched.op("sp", fn, waits=waits, inc=q_sp, n=16)
            return sched.op("act", fn, waits=waits, inc=q_act, n=16)

        wr_tok = sched.op(
            "pool",
            lambda g: g.dma_start(
                out=wr_sb[:, :],
                in_=wr_t.rearrange("(dc p) e -> p dc e", p=128)),
            inc=q_gp, n=16)
        pp_tok = {}
        for name, src, dst in (("wk", wk, wkb), ("wq", wq, wqb),
                               ("wo", wo, wob)):
            for c in range(4):
                pp_tok[(name, c)] = sched.op(
                    "pool",
                    lambda g, s=src, d=dst, cc=c: g.dma_start(
                        out=d[cc * 512:(cc + 1) * 512, :],
                        in_=s[cc * 512:(cc + 1) * 512, :]),
                    inc=q_gp, n=16)
        io_tok = sched.op(
            "pool",
            lambda g: g.iota(spare[:, 0:128], [[-1, 128]], base=0,
                             channel_multiplier=1,
                             allow_small_or_imprecise_dtypes=True),
            inc=s_pool)
        ident_tok = sched.op(
            "dve",
            lambda g: g.tensor_single_scalar(
                ident[:, :], spare[:, 0:128], 0.0, ALU.is_equal),
            waits=[io_tok], inc=s_dve)

        if phases < 1:
            _finish(sched, bsems, q_sp, out, rb_f, debug, w_dram, w_sb)
            _replay(nc, sched, strict)
            return nc

        xstg = [rc_f[:, i * 2048:(i + 1) * 2048] for i in range(2)]
        xnf32 = [rc_f[:, 4096 + i * 2048: 4096 + (i + 1) * 2048]
                 for i in range(2)]
        pstg = [rc[:, 8192 + i * 2048: 8192 + (i + 1) * 2048] for i in range(2)]
        ST = {k: i for i, k in enumerate(
            ["s1", "sq", "m", "msq", "mm", "var", "lnv", "rstd"])}

        def st(slot, k):
            c = slot * 16 + ST[k]
            return stats[:, c:c + 1]

        xstg_free = [None, None]
        xnf32_free = [None, None]
        pstg_free = [None, None]
        xn_tok = [None] * 32
        xd_tok = [None] * 32
        xc_prev = None
        tr15 = None

        for i in range(n_tiles):
            part, tt = (0, i) if i < 16 else (1, i - 16)
            src = x_mine if part == 0 else x_part
            sl = i % 2
            lx = sched.op(
                "sp",
                lambda g, s=src, t=tt, sl=sl: g.dma_start(
                    out=xstg[sl], in_=s[t * 128:(t + 1) * 128, :]),
                waits=[xstg_free[sl]], inc=q_sp, n=16)
            if "stats" in _SKIP:
                import os as _os
                if part == 0:
                    if "noap" not in _SKIP:
                        t_ap = sched.op("dve", lambda g, sl=sl: g.tensor_scalar(
                            xnf32[sl], xstg[sl], 0.0, 1.0, ALU.subtract, ALU.mult),
                            waits=[lx, xnf32_free[sl]], inc=s_dve)
                    else:
                        t_ap = sched.op("sp", lambda g, sl=sl: g.dma_start(
                            out=xnf32[sl], in_=x_mine[tt * 128:(tt + 1) * 128, :]),
                            waits=[xnf32_free[sl]], inc=q_sp, n=16)
                    if "noxnt" not in _SKIP:
                        t_cp = sched.op("dve", lambda g, sl=sl, t=tt: g.tensor_copy(
                            rb[:, t * 2048:(t + 1) * 2048], xnf32[sl]),
                            waits=[t_ap], inc=s_dve)
                    else:
                        t_cp = t_ap
                    xn_tok[i] = t_cp
                    xstg_free[sl] = t_ap
                    if "st" not in _SKIP:
                        t_st = sched.op("sp", lambda g, t=tt: g.dma_start(
                            out=xn_dram[t * 128:(t + 1) * 128, :],
                            in_=rb[:, t * 2048:(t + 1) * 2048]),
                            waits=[t_cp], inc=q_sp, n=16)
                        xd_tok[i] = t_st
                    last_tr = None
                    import os as _os
                    _ntr = int(_os.environ.get("K_NTR", str(DC)))
                    for k in range(_ntr):
                        w = [t_ap, xc_prev] if k == 0 else []
                        if i == 0 and k == 0:
                            w.append(ident_tok)
                        last_tr = sched.op(
                            "pe",
                            lambda g, sl=sl, kk=k: g.transpose(
                                psA[:, kk * 128:(kk + 1) * 128],
                                xnf32[sl][:, kk * 128:(kk + 1) * 128],
                                ident[:, :]),
                            waits=w, inc=s_pe if k == _ntr - 1 else None)
                    tr15 = last_tr
                    xnf32_free[sl] = last_tr
                    xc_prev = sched.op(
                        "dve",
                        lambda g, t=tt: g.tensor_copy(
                            bass.AP(ra, t * 128, [[32768, 128], [T, DC], [1, 128]]),
                            psA[:, 0:2048]),
                        waits=[last_tr], inc=s_dve)
                p1_sq_tok = None
                continue
            sched.op(
                "dve",
                lambda g, sl=sl: g.reduce_sum(st(sl, "s1"), xstg[sl], axis=AX.X),
                waits=[lx], inc=s_dve)
            t_sq = sched.op(
                "act",
                lambda g, sl=sl: g.activation(
                    psB[:, 0:2048], xstg[sl], AF.Square,
                    accum_out=st(sl, "sq")),
                waits=[lx], inc=s_act)
            sched.op("dve", lambda g, sl=sl: g.tensor_scalar(
                st(sl, "m"), st(sl, "s1"), KD, None, ALU.mult))
            t_msq = sched.op("dve", lambda g, sl=sl: g.tensor_scalar(
                st(sl, "msq"), st(sl, "sq"), KD, None, ALU.mult),
                waits=[t_sq])
            sched.op("dve", lambda g, sl=sl: g.tensor_mul(
                st(sl, "mm"), st(sl, "m"), st(sl, "m")))
            t_var = sched.op("dve", lambda g, sl=sl: g.tensor_scalar(
                st(sl, "var"), st(sl, "msq"), st(sl, "mm"), EPS,
                ALU.subtract, ALU.add), inc=s_dve)
            t_ln = sched.op("act", lambda g, sl=sl: g.activation(
                st(sl, "lnv"), st(sl, "var"), AF.Ln),
                waits=[t_var], inc=s_act)
            t_rstd = sched.op("act", lambda g, sl=sl: g.activation(
                st(sl, "rstd"), st(sl, "lnv"), AF.Exp, scale=-0.5),
                inc=s_act)
            if part == 0:
                t_ap = sched.op("dve", lambda g, sl=sl: g.tensor_scalar(
                    xnf32[sl], xstg[sl], st(sl, "m"), st(sl, "rstd"),
                    ALU.subtract, ALU.mult),
                    waits=[t_rstd, xnf32_free[sl]], inc=s_dve)
                t_cp = sched.op("dve", lambda g, sl=sl, t=tt: g.tensor_copy(
                    rb[:, t * 2048:(t + 1) * 2048], xnf32[sl]), inc=s_dve)
                xn_tok[i] = t_cp
                xstg_free[sl] = t_ap
                if "st" not in _SKIP:
                    t_st = sched.op("sp", lambda g, t=tt: g.dma_start(
                        out=xn_dram[t * 128:(t + 1) * 128, :],
                        in_=rb[:, t * 2048:(t + 1) * 2048]),
                        waits=[t_cp], inc=q_sp, n=16)
                    xd_tok[i] = t_st
                last_tr = None
                import os as _os
                _ntr = int(_os.environ.get("K_NTR", str(DC)))
                for k in range(_ntr if ("tp" not in _SKIP and "cponly" not in _SKIP) else 0):
                    w = [t_ap, xc_prev] if k == 0 else []
                    if i == 0 and k == 0:
                        w.append(ident_tok)
                    last_tr = sched.op(
                        "pe",
                        lambda g, sl=sl, kk=k: g.transpose(
                            psA[:, kk * 128:(kk + 1) * 128],
                            xnf32[sl][:, kk * 128:(kk + 1) * 128],
                            ident[:, :]),
                        waits=w, inc=s_pe if k == DC - 1 else None)
                tr15 = last_tr
                xnf32_free[sl] = last_tr
                if "cponly" in _SKIP:
                    xc_prev = sched.op(
                        "dve",
                        lambda g, t=tt: g.tensor_copy(
                            ra[:, 0:2048], psA[:, 0:2048]),
                        waits=[t_ap], inc=s_dve)
                elif "tp" not in _SKIP and "noxc" not in _SKIP:
                    if "xc" in _SKIP:
                        xc_prev = sched.op(
                            "dve",
                            lambda g, t=tt: g.tensor_copy(
                                ra[:, 0:2048], psA[:, 0:2048]),
                            waits=[last_tr], inc=s_dve)
                    else:
                        xc_prev = sched.op(
                            "dve",
                            lambda g, t=tt: g.tensor_copy(
                                bass.AP(ra, t * 128, [[32768, 128], [T, DC], [1, 128]]),
                                psA[:, 0:2048]),
                            waits=[last_tr], inc=s_dve)
            else:
                w = [t_rstd, pstg_free[sl]]
                if i == 16:
                    w.append(tr15)
                t_ap = sched.op("dve", lambda g, sl=sl: g.tensor_scalar(
                    pstg[sl], xstg[sl], st(sl, "m"), st(sl, "rstd"),
                    ALU.subtract, ALU.mult), waits=w, inc=s_dve)
                xn_tok[i] = t_ap
                xstg_free[sl] = t_ap
                t_st = sched.op("sp", lambda g, sl=sl, t=tt: g.dma_start(
                    out=xn_dram[T + t * 128: T + (t + 1) * 128, :],
                    in_=pstg[sl]),
                    waits=[t_ap], inc=q_sp, n=16)
                xd_tok[i] = t_st
                pstg_free[sl] = t_st
            p1_sq_tok = t_sq

        tr_tok = xc_prev

        if phases < 2:
            _finish(sched, bsems, q_sp, out, rb_f, debug, w_dram, w_sb)
            _replay(nc, sched, strict)
            return nc

        PSL = [12288, 14336, 16384, 18432, 0, 2048, 4096, 6144, 8192, 10240]
        pring = [rc[:, off: off + 2048] for off in PSL]
        first_guard = [None, None, None, None,
                       xstg_free[0], xstg_free[0], xstg_free[1], xstg_free[1],
                       pstg_free[0], pstg_free[1]]
        gstg = [rc[:, 20480 + i * 2048: 20480 + (i + 1) * 2048]
                for i in range(2)]
        pl_tok = {}
        cons_tok = {}
        gstg_free = [None, None]
        prev_evac = [None, None]

        def emit_pload(li, guard):
            p, ptt = divmod(li, 16)
            pl_tok[(p, ptt)] = hw_dma(
                li,
                lambda g, pt=ptt, sl=li % 10: g.dma_start(
                    out=pring[sl],
                    in_=xn_dram[T + pt * 128: T + (pt + 1) * 128, :]),
                waits=[xd_tok[16 + ptt], guard])

        for li in range(10):
            emit_pload(li, first_guard[li])
        for p in range(8):
            for tile in range(32):
                if tile < 16:
                    src_t = rb[:, tile * 2048:(tile + 1) * 2048]
                    w = [xn_tok[tile]] if p == 0 else []
                else:
                    src_t = pring[(p * 16 + tile - 16) % 10]
                    w = [pl_tok[(p, tile - 16)]]
                if tile == 0:
                    w = w + [prev_evac[0], prev_evac[1]]
                    if p == 0:
                        w = w + [p1_sq_tok, tr_tok]
                last = None
                for ei, ec in enumerate((2 * p, 2 * p + 1)):
                    t = "A" if ei == 0 else "B"
                    for fb in range(4):
                        is_last = (ei == 1 and fb == 3)
                        last = sched.op(
                            "pe",
                            lambda g, s=src_t, e=ec, tt_=t, f=fb, ti=tile:
                                g.matmul(
                                    ps(tt_, f), s[:, e * 128:(e + 1) * 128],
                                    s[:, f * 512:(f + 1) * 512],
                                    start=(ti == 0), stop=(ti == 31)),
                            waits=w if (ei == 0 and fb == 0) else [],
                            inc=s_pe if (is_last and (tile >= 16 or tile == 31))
                                else None)
                if tile >= 16:
                    k = tile - 16
                    cons_tok[(p, k)] = last
                    li_next = p * 16 + k + 10
                    if li_next < 128:
                        emit_pload(li_next, last)
            stop_tok = last
            for ei in range(2):
                t = "A" if ei == 0 else "B"
                w = [stop_tok, gstg_free[ei]]
                ev = sched.op(
                    "dve",
                    lambda g, tt_=t, sl=ei: g.tensor_copy(
                        gstg[sl], ps(tt_, 0, 4)),
                    waits=w, inc=s_dve)
                prev_evac[ei] = ev
                sc = 2 * p + ei
                stok = sched.op(
                    "sp",
                    lambda g, sl=ei, s=sc: g.dma_start(
                        out=g_dram[s], in_=gstg[sl]),
                    waits=[ev], inc=q_sp, n=16)
                gstg_free[ei] = stok

        for i in range(4):
            bank[("A", i)]["r"] = prev_evac[0]
            bank[("B", i)]["r"] = prev_evac[1]

        RS = {k: i for i, k in enumerate(
            ["ls", "e1", "mask", "l3", "sel", "w1"])}

        def rs(k):
            c = RS[k] * 8
            return rsm[:, c:c + 8]

        def rps(tt):
            t = "A" if tt % 8 < 4 else "B"
            b = tt % 4
            off = (tt // 8) * 8
            tens = psA if t == "A" else psB
            return (t, b), tens[:, b * 512 + off: b * 512 + off + 8]

        RSM1 = rsm[:, 48:49]
        RSM2 = rsm[:, 49:50]
        RSM3 = rsm[:, 50:51]
        RSM4 = rsm[:, 51:52]
        rt_done = None
        t_e1_prev = None
        r_mm_tok = [None] * NTT
        r_cp_tok = [None] * NTT
        for tt in range(NTT):
            key, slc = rps(tt)
            for dc_ in range(DC):
                w = []
                if dc_ == 0:
                    w = [bank[key]["r"]]
                    if tt == 0:
                        w += [tr_tok, wr_tok, xn_tok[15]]
                r_mm_tok[tt] = sched.op(
                    "pe",
                    lambda g, t=tt, d=dc_, s=slc: g.matmul(
                        s, xnf_s(d, t * 128, (t + 1) * 128),
                        wr_sb[:, d * 8:(d + 1) * 8],
                        start=(d == 0), stop=(d == DC - 1)),
                    waits=w, inc=s_pe if dc_ == DC - 1 else None)
            bank[key]["w"] = r_mm_tok[tt]
            r_cp_tok[tt] = sched.op(
                "dve", lambda g, s=slc: g.tensor_copy(rs("ls"), s),
                waits=[r_mm_tok[tt], t_e1_prev], inc=s_dve)
            bank[key]["r"] = r_cp_tok[tt]
            sched.op("dve", lambda g: g.reduce_max(
                RSM1, rs("ls"), axis=AX.X, negate=True),
                waits=[t_e1_prev], inc=s_dve)
            t_e1 = sched.op("act", lambda g: g.activation(
                rs("e1"), rs("ls"), AF.Exp, bias=RSM1, accum_out=RSM2),
                waits=[(s_dve, s_dve.count)], inc=s_act)
            t_e1_prev = t_e1
            sched.op("dve", lambda g: g.reciprocal(RSM3, RSM2),
                     waits=[t_e1])
            sched.op("dve", lambda g: g.tensor_scalar(
                rs("mask"), rs("ls"), RSM1, 0.0, ALU.add, ALU.is_ge))
            sched.op("dve", lambda g: g.scalar_tensor_tensor(
                rs("l3"), rs("mask"), -1e9, rs("ls"), ALU.mult, ALU.add))
            sched.op("dve", lambda g: g.reduce_max(
                RSM4, rs("l3"), axis=AX.X, negate=True))
            sched.op("dve", lambda g: g.tensor_scalar(
                rs("sel"), rs("ls"), RSM4, 0.0, ALU.add, ALU.is_ge))
            sched.op("dve", lambda g: g.tensor_scalar(
                rs("w1"), rs("e1"), RSM3, None, ALU.mult))
            rt_done = sched.op("dve", lambda g, t=tt: g.tensor_mul(
                w_sb[:, t * 8:(t + 1) * 8], rs("w1"), rs("sel")), inc=s_dve)

        wvb = [rc[:, i * 8192:(i + 1) * 8192] for i in range(2)]
        vstg = [rc[:, 16384 + i * 512: 16384 + (i + 1) * 512] for i in range(4)]
        wv_tok = [None] * 4
        wvb_free = [None, None]
        vstg_free = [None] * 4
        v_last = [None] * 4
        vs_idx = 0
        wv3 = wv.rearrange("(ec p) f -> p ec f", p=128)
        gram_done = stop_tok
        slot_i = 0
        for gidx in range(4):
            wv_tok[gidx] = sched.op(
                "pool",
                lambda g, gi=gidx: g.dma_start(
                    out=bass.AP(rc, (gi % 2) * 8192,
                                [[32768, 128], [512, DC], [1, 512]]),
                    in_=wv3[:, :, gi * 512:(gi + 1) * 512]),
                waits=[wvb_free[gidx % 2], gram_done], inc=q_gp, n=16)
            for j in range(4):
                fvc = gidx * 4 + j
                for tb in range(4):
                    sl = slot_i % 8
                    t, i = ("A", sl) if sl < 4 else ("B", sl - 4)
                    last = None
                    for ec in range(DC):
                        w = []
                        if ec == 0:
                            w = [wv_tok[gidx], bank[(t, i)]["r"]]
                            if gidx == 0 and j == 0 and tb == 0:
                                w.append(rt_done)
                        last = sched.op(
                            "pe",
                            lambda g, tt_=t, ii=i, e=ec, gi=gidx, jj=j, b=tb:
                                g.matmul(
                                    ps(tt_, ii),
                                    wvb[gi % 2][:, e * 512 + jj * 128:
                                                e * 512 + (jj + 1) * 128],
                                    xnf_s(e, b * 512, (b + 1) * 512),
                                    start=(e == 0), stop=(e == DC - 1)),
                            waits=w, inc=s_pe if ec == DC - 1 else None)
                    bank[(t, i)]["w"] = last
                    if fvc == gidx * 4 + 3 and tb == 3:
                        wvb_free[gidx % 2] = last
                    vsl = vs_idx % 4
                    vs_idx += 1
                    ev = sched.op(
                        "dve",
                        lambda g, tt_=t, ii=i, vs=vsl: g.tensor_copy(
                            vstg[vs], ps(tt_, ii)),
                        waits=[last, vstg_free[vsl]], inc=s_dve)
                    bank[(t, i)]["r"] = ev
                    stok = sched.op(
                        "sp",
                        lambda g, vs=vsl, fv=fvc, b=tb: g.dma_start(
                            out=v_dram[fv, :, b * 512:(b + 1) * 512],
                            in_=vstg[vs]),
                        waits=[ev], inc=q_sp, n=16)
                    vstg_free[vsl] = stok
                    if fvc == 15:
                        v_last[tb] = stok
                    slot_i += 1

        sched.barrier(bsems)
        if phases < 3:
            _finish(sched, bsems, q_sp, out, rb_f, debug, w_dram, w_sb)
            _replay(nc, sched, strict)
            return nc

        HF0 = 0
        WD0 = 11264
        WG0 = 22528
        WU0 = 26624
        GFF0 = 30720

        def hf(fc_, t0, t1):
            return rc[:, HF0 + fc_ * 1024 + t0: HF0 + fc_ * 1024 + t1]

        wd_ring = [rc[:, WD0 + i * 5632: WD0 + (i + 1) * 5632] for i in range(2)]
        wg_ring = [rc[:, WG0 + i * 2048: WG0 + (i + 1) * 2048] for i in range(2)]
        wu_ring = [rc[:, WU0 + i * 2048: WU0 + (i + 1) * 2048] for i in range(2)]
        gffs = [rc[:, GFF0:GFF0 + 2048], rc[:, GFF0:GFF0 + 2048]]
        gs = [spare[:, i * 512:(i + 1) * 512] for i in range(4)]

        experts = [("sh", 0), ("sh", 1)] + [("ex", e) for e in range(NEXP)]
        pairs = [(0, 1), (2, 3), (4, 5), (6, 7), (8, 9), (10,)]

        gs_free = [None] * 4
        gffs_free = [None, None]
        out_guard = [None] * 16
        g_echo, u_echo, d_echo = {}, {}, {}
        g_echo_by_gi, u_echo_by_gi, d_echo_by_gi = {}, {}, {}
        wg_cons = {}
        wu_cons = {}
        wd_cons = {}
        wgi = 0
        wdi = 0
        gff_store = [None] * 16

        for th in range(2):
            for exi, (kind, eidx) in enumerate(experts):
                h_toks = []
                for pi, pr in enumerate(pairs):
                    gtoks = {}
                    utoks = {}
                    for fc_ in pr:
                        gi = wgi + fc_
                        sl = gi % 2
                        ckey = (kind, eidx, fc_)
                        if kind == "sh":
                            srcg = wsg_c[eidx * FC + fc_]
                            srcu = wsu_c[eidx * FC + fc_]
                            dstg = wsgb[eidx * FC + fc_]
                            dstu = wsub[eidx * FC + fc_]
                        else:
                            srcg = wg_c[eidx, fc_]
                            srcu = wu_c[eidx, fc_]
                            dstg = wgb[eidx, fc_]
                            dstu = wub[eidx, fc_]
                        if th == 0:
                            gtoks[fc_] = sched.op(
                                "pool",
                                lambda g, s=srcg, sl=sl: g.dma_start(
                                    out=wg_ring[sl],
                                    in_=s.rearrange("p dc ff -> p (dc ff)")),
                                waits=[wg_cons.get(gi - 2),
                                       g_echo_by_gi.get(gi - 2)],
                                inc=q_gp, n=16)
                            g_echo[ckey] = g_echo_by_gi[gi] = sched.op(
                                "sp",
                                lambda g, d=dstg, sl=sl: g.dma_start(
                                    out=d, in_=wg_ring[sl]),
                                waits=[gtoks[fc_]], inc=q_sp, n=16)
                            utoks[fc_] = sched.op(
                                "pool",
                                lambda g, s=srcu, sl=sl: g.dma_start(
                                    out=wu_ring[sl],
                                    in_=s.rearrange("p dc ff -> p (dc ff)")),
                                waits=[wu_cons.get(gi - 2),
                                       u_echo_by_gi.get(gi - 2)],
                                inc=q_gp, n=16)
                            u_echo[ckey] = u_echo_by_gi[gi] = sched.op(
                                "sp",
                                lambda g, d=dstu, sl=sl: g.dma_start(
                                    out=d, in_=wu_ring[sl]),
                                waits=[utoks[fc_]], inc=q_sp, n=16)
                        else:
                            gtoks[fc_] = sched.op(
                                "act",
                                lambda g, d=dstg, sl=sl: g.dma_start(
                                    out=wg_ring[sl], in_=d),
                                waits=[wg_cons.get(gi - 2),
                                       g_echo_by_gi.get(gi - 2),
                                       g_echo[ckey]],
                                inc=q_act, n=16)
                            utoks[fc_] = sched.op(
                                "act",
                                lambda g, d=dstu, sl=sl: g.dma_start(
                                    out=wu_ring[sl], in_=d),
                                waits=[wu_cons.get(gi - 2),
                                       u_echo_by_gi.get(gi - 2),
                                       u_echo[ckey]],
                                inc=q_act, n=16)
                    gstop = {}
                    for si, fc_ in enumerate(pr):
                        wsl = (wgi + fc_) % 2
                        for tb in range(2):
                            slot = si * 2 + tb
                            last = None
                            for dc_ in range(DC):
                                w = []
                                if dc_ == 0:
                                    w = [gtoks[fc_], bank[("A", slot)]["r"]]
                                last = sched.op(
                                    "pe",
                                    lambda g, sl=wsl, d=dc_, b=tb,
                                           so=slot, t=th:
                                        g.matmul(
                                            ps("A", so),
                                            wg_ring[sl][:, d * 128:(d + 1) * 128],
                                            xnf_s(d, t * 1024 + b * 512,
                                                  t * 1024 + (b + 1) * 512),
                                            start=(d == 0), stop=(d == DC - 1)),
                                    waits=w, inc=s_pe if dc_ == DC - 1 else None)
                            bank[("A", slot)]["w"] = last
                            gstop[slot] = last
                        wg_cons[wgi + fc_] = last
                    ustop = {}
                    for si, fc_ in enumerate(pr):
                        wsl = (wgi + fc_) % 2
                        for tb in range(2):
                            slot = si * 2 + tb
                            last = None
                            for dc_ in range(DC):
                                w = []
                                if dc_ == 0:
                                    w = [utoks[fc_], bank[("B", slot)]["r"]]
                                last = sched.op(
                                    "pe",
                                    lambda g, sl=wsl, d=dc_, b=tb,
                                           so=slot, t=th:
                                        g.matmul(
                                            ps("B", so),
                                            wu_ring[sl][:, d * 128:(d + 1) * 128],
                                            xnf_s(d, t * 1024 + b * 512,
                                                  t * 1024 + (b + 1) * 512),
                                            start=(d == 0), stop=(d == DC - 1)),
                                    waits=w, inc=s_pe if dc_ == DC - 1 else None)
                            bank[("B", slot)]["w"] = last
                            ustop[slot] = last
                        wu_cons[wgi + fc_] = last
                    for si, fc_ in enumerate(pr):
                        for tb in range(2):
                            slot = si * 2 + tb
                            t_sil = sched.op(
                                "act",
                                lambda g, so=slot: g.activation(
                                    gs[so], ps("A", so), AF.Silu),
                                waits=[gstop[slot], gs_free[slot]],
                                inc=s_act)
                            bank[("A", slot)]["r"] = t_sil
                            t_h = sched.op(
                                "dve",
                                lambda g, so=slot, f=fc_, b=tb: g.tensor_tensor(
                                    hf(f, b * 512, (b + 1) * 512),
                                    gs[so], ps("B", so), ALU.mult),
                                waits=[ustop[slot], t_sil], inc=s_dve)
                            bank[("B", slot)]["r"] = t_h
                            gs_free[slot] = t_h
                            h_toks.append(t_h)
                wgi += FC

                h_all = h_toks[-1]
                for db in range(4):
                    gi = wdi + db
                    wsl = gi % 2
                    dkey = (kind, eidx, db)
                    if kind == "sh":
                        srcd = wsd_t[eidx * F:(eidx + 1) * F,
                                     db * 512:(db + 1) * 512]
                        dstd = wsdb[eidx, db]
                    else:
                        srcd = wd_t[eidx][:, db * 512:(db + 1) * 512]
                        dstd = wdb[eidx, db]
                    if th == 0:
                        wd_tok = sched.op(
                            "pool",
                            lambda g, s=srcd, sl=wsl: g.dma_start(
                                out=wd_ring[sl],
                                in_=s.rearrange("(fc p) d -> p fc d", p=128)),
                            waits=[wd_cons.get(gi - 2),
                                   d_echo_by_gi.get(gi - 2)],
                            inc=q_gp, n=16)
                        d_echo[dkey] = d_echo_by_gi[gi] = sched.op(
                            "sp",
                            lambda g, d=dstd, sl=wsl: g.dma_start(
                                out=d, in_=wd_ring[sl]),
                            waits=[wd_tok], inc=q_sp, n=16)
                    else:
                        wd_tok = sched.op(
                            "act",
                            lambda g, d=dstd, sl=wsl: g.dma_start(
                                out=wd_ring[sl], in_=d),
                            waits=[wd_cons.get(gi - 2),
                                   d_echo_by_gi.get(gi - 2),
                                   d_echo[dkey]],
                            inc=q_act, n=16)
                    ev_toks = []
                    for tt in range(8):
                        t, i = ("A", tt) if tt < 4 else ("B", tt - 4)
                        last = None
                        for fc_ in range(FC):
                            w = []
                            if fc_ == 0:
                                w = [wd_tok, bank[(t, i)]["r"], h_all]
                            last = sched.op(
                                "pe",
                                lambda g, sl=wsl, f=fc_, tt_=t, ii=i, tl=tt:
                                    g.matmul(
                                        ps(tt_, ii),
                                        hf(f, tl * 128, (tl + 1) * 128),
                                        wd_ring[sl][:, f * 512:(f + 1) * 512],
                                        start=(f == 0), stop=(f == FC - 1)),
                                waits=w, inc=s_pe if fc_ == FC - 1 else None)
                        bank[(t, i)]["w"] = last
                        if tt == 7:
                            wd_cons[gi] = last
                        gtt = th * 8 + tt
                        oslice = rb_f[:, tt * 2048 + db * 512:
                                      tt * 2048 + (db + 1) * 512]
                        w = [last, out_guard[gtt]]
                        if exi == 0:
                            ev = sched.op(
                                "dve",
                                lambda g, o=oslice, tt_=t, ii=i: g.tensor_copy(
                                    o, ps(tt_, ii)),
                                waits=w, inc=s_dve)
                        elif kind == "sh":
                            ev = sched.op(
                                "dve",
                                lambda g, o=oslice, tt_=t, ii=i: g.tensor_add(
                                    o, o, ps(tt_, ii)),
                                waits=w, inc=s_dve)
                        else:
                            wcol = w_sb[:, gtt * 8 + eidx: gtt * 8 + eidx + 1]
                            ev = sched.op(
                                "dve",
                                lambda g, o=oslice, tt_=t, ii=i, wc=wcol:
                                    g.scalar_tensor_tensor(
                                        o, ps(tt_, ii), wc, o,
                                        ALU.mult, ALU.add),
                                waits=w, inc=s_dve)
                        bank[(t, i)]["r"] = ev
                        ev_toks.append(ev)
                wdi += 4
                if exi == len(experts) - 1:
                    for tt in range(8):
                        gtt = th * 8 + tt
                        sl = 0
                        t_g = sched.op(
                            "act",
                            lambda g, tl=tt, sl=sl: g.activation(
                                gffs[sl], rb_f[:, tl * 2048:(tl + 1) * 2048],
                                AF.Silu),
                            waits=[ev_toks[-1], gffs_free[sl]],
                            inc=s_act)
                        stok = sched.op(
                            "sp",
                            lambda g, sl=sl, gt=gtt: g.dma_start(
                                out=gff_dram[gt * 128:(gt + 1) * 128, :],
                                in_=gffs[sl]),
                            waits=[t_g], inc=q_sp, n=16)
                        gffs_free[sl] = stok
                        gff_store[gtt] = stok
                        out_guard[gtt] = t_g

        up_last = wu_cons[wgi - 1]
        g_tok = [None] * DC
        wk_tok = [None] * DC
        for sc in range(DC):
            g_tok[sc] = hw_dma(
                sc, lambda g, s=sc: g.dma_start(
                    out=ra[:, s * 2048:(s + 1) * 2048], in_=g_dram[s]),
                waits=[up_last])
        for sc in range(DC):
            wk_tok[sc] = hw_dma(
                sc + 1, lambda g, s=sc: g.dma_start(
                    out=rb[:, s * 2048:(s + 1) * 2048],
                    in_=wkb[s * 128:(s + 1) * 128, :]),
                waits=[out_guard[15], pp_tok[("wk", sc // 4)]])

        if phases < 4:
            _finish(sched, bsems, q_sp, out, rb_f, debug, w_dram, w_sb)
            _replay(nc, sched, strict)
            return nc

        def slab_pass(lhs_slab, rhs_slab, dst_slab, n_out, k_loads,
                      extra_first=()):
            evs = []
            last = None
            for oc in range(n_out):
                t = "A" if oc % 2 == 0 else "B"
                for k in range(DC):
                    for cb in range(4):
                        w = []
                        if k == 0 and cb == 0:
                            w = list(extra_first) + [bank[(t, 0)]["r"],
                                                     bank[(t, 1)]["r"],
                                                     bank[(t, 2)]["r"],
                                                     bank[(t, 3)]["r"]]
                        if oc == 0 and cb == 0:
                            w = w + list(k_loads[k])
                        last = sched.op(
                            "pe",
                            lambda g, tt_=t, kk=k, c=cb, o=oc: g.matmul(
                                ps(tt_, c), lhs_slab(kk, o), rhs_slab(kk, c),
                                start=(kk == 0), stop=(kk == DC - 1)),
                            waits=w,
                            inc=s_pe if (k == DC - 1 and cb == 3) else None)
                for i in range(4):
                    bank[(t, i)]["w"] = last
                ev = sched.op(
                    "dve",
                    lambda g, tt_=t, o=oc: g.tensor_copy(
                        dst_slab(o), ps(tt_, 0, 4)),
                    waits=[last], inc=s_dve)
                for i in range(4):
                    bank[(t, i)]["r"] = ev
                evs.append(ev)
            return evs, last

        r_evs, p4_last = slab_pass(
            lhs_slab=lambda k, o: ra[:, k * 2048 + o * 128: k * 2048 + (o + 1) * 128],
            rhs_slab=lambda k, c: rb[:, k * 2048 + c * 512: k * 2048 + (c + 1) * 512],
            dst_slab=lambda o: rc[:, o * 2048:(o + 1) * 2048],
            n_out=DC, k_loads=[[g_tok[k], wk_tok[k]] for k in range(DC)],
            extra_first=[gff_store[15]])

        if phases < 5:
            if debug:
                for o in range(DC):
                    sched.op("sp", lambda g, oo=o: g.dma_start(
                        out=r_dram[oo], in_=rc[:, oo * 2048:(oo + 1) * 2048]),
                        waits=[r_evs[o]], inc=q_sp, n=16)
            _finish(sched, bsems, q_sp, out, rb_f, debug, w_dram, w_sb)
            _replay(nc, sched, strict)
            return nc

        wq_tok = [None] * DC
        for sc in range(DC):
            wq_tok[sc] = hw_dma(
                sc, lambda g, s=sc: g.dma_start(
                    out=ra[:, s * 2048:(s + 1) * 2048],
                    in_=wqb[s * 128:(s + 1) * 128, :]),
                waits=[p4_last, pp_tok[("wq", sc // 4)]])

        NM = rsm[:, 52:53]
        NM8 = rsm[:, 53:54]
        SSUM = rsm[:, 54:55]
        RSUM = rsm[:, 55:56]
        w_evs = []
        for ac in range(DC):
            t = "A" if ac % 2 == 0 else "B"
            last = None
            for k in range(DC):
                for cb in range(4):
                    w = []
                    if k == 0 and cb == 0:
                        w = [bank[(t, 0)]["r"], bank[(t, 1)]["r"],
                             bank[(t, 2)]["r"], bank[(t, 3)]["r"]]
                    if ac == 0 and cb == 0:
                        w = w + [wq_tok[k]]
                    last = sched.op(
                        "pe",
                        lambda g, tt_=t, kk=k, c=cb, o=ac: g.matmul(
                            ps(tt_, c),
                            ra[:, kk * 2048 + o * 128: kk * 2048 + (o + 1) * 128],
                            rc[:, kk * 2048 + c * 512: kk * 2048 + (c + 1) * 512],
                            start=(kk == 0), stop=(kk == DC - 1)),
                        waits=w,
                        inc=s_pe if (k == DC - 1 and cb == 3) else None)
            for i in range(4):
                bank[(t, i)]["w"] = last
            t_nm = sched.op("dve", lambda g, tt_=t: g.reduce_max(
                NM, ps(tt_, 0, 4), axis=AX.X, negate=True),
                waits=[last, w_evs[-1] if w_evs else None], inc=s_dve)
            sched.op("dve", lambda g: g.tensor_scalar(
                NM8, NM, 0.125, None, ALU.mult), inc=s_dve)
            t_exp = sched.op("act", lambda g, tt_=t: g.activation(
                spare[:, 0:2048], ps(tt_, 0, 4), AF.Exp,
                scale=0.125, bias=NM8, accum_out=SSUM),
                waits=[(s_dve, s_dve.count)], inc=s_act)
            for i in range(4):
                bank[(t, i)]["r"] = t_exp
            t_rs = sched.op("dve", lambda g: g.reciprocal(RSUM, SSUM),
                            waits=[t_exp], inc=s_dve)
            ev = sched.op("dve", lambda g, o=ac: g.tensor_scalar(
                rb[:, o * 2048:(o + 1) * 2048], spare[:, 0:2048],
                RSUM, None, ALU.mult), waits=[t_rs], inc=s_dve)
            w_evs.append(ev)
        p5_last = last

        if phases < 6:
            if debug:
                for o in range(DC):
                    sched.op("sp", lambda g, oo=o: g.dma_start(
                        out=aw_dram[oo], in_=rb[:, oo * 2048:(oo + 1) * 2048]),
                        waits=[w_evs[o]], inc=q_sp, n=16)
            _finish(sched, bsems, q_sp, out, rb_f, debug, w_dram, w_sb)
            _replay(nc, sched, strict)
            return nc

        wo_tok = [None] * DC
        for sc in range(DC):
            wo_tok[sc] = hw_dma(
                sc, lambda g, s=sc: g.dma_start(
                    out=rc[:, s * 2048:(s + 1) * 2048],
                    in_=wob[s * 128:(s + 1) * 128, :]),
                waits=[p5_last, pp_tok[("wo", sc // 4)]])

        m_evs, p6_last = slab_pass(
            lhs_slab=lambda k, o: rb[:, k * 2048 + o * 128: k * 2048 + (o + 1) * 128],
            rhs_slab=lambda k, c: rc[:, k * 2048 + c * 512: k * 2048 + (c + 1) * 512],
            dst_slab=lambda o: ra[:, o * 2048:(o + 1) * 2048],
            n_out=DC, k_loads=[[w_evs[k], wo_tok[k]] for k in range(DC)])

        vring = [rc[:, i * 8192:(i + 1) * 8192] for i in range(2)]
        gfr = [rc[:, 16384 + i * 2048: 16384 + (i + 1) * 2048]
               for i in range(2)]
        xr = [rb_f[:, i * 2048:(i + 1) * 2048] for i in range(2)]
        fstg = [rb_f[:, 4096 + i * 2048: 4096 + (i + 1) * 2048]
                for i in range(2)]
        vring_free = [None, None]
        gfr_free = [None, None]
        xr_free = [None, None]
        fstg_free = [None, None]
        vg_tok = [None] * 4

        def emit_vload(gi_, guard):
            vg_tok[gi_] = hw_dma(
                gi_, lambda g, gg=gi_: g.dma_start(
                    out=vring[gg % 2],
                    in_=v_dram[:, :, gg * 512:(gg + 1) * 512].rearrange(
                        "fc p t -> p fc t")),
                waits=[guard, p6_last if gi_ < 2 else None, v_last[gi_]])

        emit_vload(0, None)
        emit_vload(1, None)
        out_toks = []
        for tt in range(NTT):
            sl = tt % 2
            first = p6_last if tt < 2 else None
            vsl = (tt // 4) % 2
            gt = sched.op(
                "sp", lambda g, t=tt, sl=sl: g.dma_start(
                    out=gfr[sl], in_=gff_dram[t * 128:(t + 1) * 128, :]),
                waits=[gfr_free[sl], first, gff_store[tt]], inc=q_sp, n=16)
            xt = sched.op(
                "sp", lambda g, t=tt, sl=sl: g.dma_start(
                    out=xr[sl], in_=x_mine[t * 128:(t + 1) * 128, :]),
                waits=[xr_free[sl], first], inc=q_sp, n=16)
            t_ = "A" if sl == 0 else "B"
            last = None
            for fc_ in range(DC):
                for gb in range(4):
                    w = []
                    if fc_ == 0 and gb == 0:
                        w = [vg_tok[tt // 4], bank[(t_, 0)]["r"],
                             bank[(t_, 1)]["r"], bank[(t_, 2)]["r"],
                             bank[(t_, 3)]["r"]]
                    if tt == 0 and gb == 0:
                        w = w + [m_evs[fc_]]
                    last = sched.op(
                        "pe",
                        lambda g, tt_=t_, k=fc_, c=gb, vs=vsl, st=tt % 4:
                            g.matmul(
                                ps(tt_, c),
                                vring[vs][:, k * 512 + st * 128:
                                          k * 512 + (st + 1) * 128],
                                ra[:, k * 2048 + c * 512: k * 2048 + (c + 1) * 512],
                                start=(k == 0), stop=(k == DC - 1)),
                        waits=w,
                        inc=s_pe if (fc_ == DC - 1 and gb == 3) else None)
            for i in range(4):
                bank[(t_, i)]["w"] = last
            if tt % 4 == 3:
                vring_free[(tt // 4) % 2] = last
                if tt // 4 + 2 < 4:
                    emit_vload(tt // 4 + 2, last)
            t1 = sched.op(
                "dve", lambda g, tt_=t_, sl=sl: g.tensor_tensor(
                    fstg[sl], ps(tt_, 0, 4), gfr[sl], ALU.mult),
                waits=[last, gt, fstg_free[sl]], inc=s_dve)
            for i in range(4):
                bank[(t_, i)]["r"] = t1
            gfr_free[sl] = t1
            t2 = sched.op(
                "dve", lambda g, sl=sl: g.tensor_add(
                    fstg[sl], fstg[sl], xr[sl]),
                waits=[xt], inc=s_dve)
            xr_free[sl] = t2
            ot = sched.op(
                "sp", lambda g, t=tt, sl=sl: g.dma_start(
                    out=out[t * 128:(t + 1) * 128, :], in_=fstg[sl]),
                waits=[t2], inc=q_sp, n=16)
            fstg_free[sl] = ot
            out_toks.append(ot)

        if debug:
            sched.op("sp", lambda g: g.dma_start(out=w_dram, in_=w_sb[:, :]),
                     waits=[rt_done], inc=q_sp, n=16)
            for o in range(DC):
                sched.op("sp", lambda g, oo=o: g.dma_start(
                    out=m_dram[oo], in_=ra[:, oo * 2048:(oo + 1) * 2048]),
                    waits=[m_evs[o]], inc=q_sp, n=16)

        sched.op("pool", lambda g: g.nop(),
                 waits=[(q_sp, q_sp.count), (q_act, q_act.count)])
        _replay(nc, sched, strict)

    return nc


def _finish(sched, bsems, q_sp, out, rb_f, debug, w_dram, w_sb):
    import os
    if not os.environ.get("K_NOBAR"):
        sched.barrier(bsems)
    for tt in range(NTT):
        tok = sched.op(
            "sp", lambda g, t=tt: g.dma_start(
                out=out[t * 128:(t + 1) * 128, :], in_=rb_f[:, 0:2048]),
            inc=q_sp, n=16)
    if debug:
        sched.op("sp", lambda g: g.dma_start(out=w_dram, in_=w_sb[:, :]),
                 inc=q_sp, n=16)
    sched.op("pool", lambda g: g.nop(), waits=[(q_sp, q_sp.count)])


def _replay(nc, sched, strict=False):
    import os
    ss = {}
    dev_chain = () if os.environ.get("K_NOCHAIN") else ("dve", "act")
    for e in (Sched.ENGINES if strict else dev_chain):
        ss[e] = nc.alloc_semaphore("strict_" + e)
    with nc.Block() as block:
        @block.tensor
        def _(eng):
            sched.replay("pe", eng, ss.get("pe"))

        @block.vector
        def _(eng):
            sched.replay("dve", eng, ss.get("dve"))

        @block.scalar
        def _(eng):
            sched.replay("act", eng, ss.get("act"))

        @block.gpsimd
        def _(eng):
            sched.replay("pool", eng, ss.get("pool"))

        @block.sync
        def _(eng):
            sched.replay("sp", eng, ss.get("sp"))


def prep_inputs(L0_attn_x, L0_attn_Wq, L0_attn_Wk, L0_attn_Wv, L0_attn_Wo,
                L0_ffn_Wr, L0_ffn_Wg, L0_ffn_Wu, L0_ffn_Wd, L0_ffn_Wsg,
                L0_ffn_Wsu, L0_ffn_Wsd):
    asc = np.ascontiguousarray
    x = np.asarray(L0_attn_x, dtype=np.float32).reshape(4, 4096, D)
    common = {
        "wq": asc(np.asarray(L0_attn_Wq, np.float32)),
        "wk": asc(np.asarray(L0_attn_Wk, np.float32)),
        "wv": asc(np.asarray(L0_attn_Wv, np.float32)),
        "wo": asc(np.asarray(L0_attn_Wo, np.float32)),
        "wr_t": asc(np.asarray(L0_ffn_Wr, np.float32).T),
        "wsg_c": asc(np.asarray(L0_ffn_Wsg, np.float32)
                     .reshape(22, 128, DC, 128).transpose(0, 3, 2, 1)),
        "wsu_c": asc(np.asarray(L0_ffn_Wsu, np.float32)
                     .reshape(22, 128, DC, 128).transpose(0, 3, 2, 1)),
        "wg_c": asc(np.asarray(L0_ffn_Wg, np.float32)
                    .reshape(NEXP, FC, 128, DC, 128).transpose(0, 1, 4, 3, 2)),
        "wu_c": asc(np.asarray(L0_ffn_Wu, np.float32)
                    .reshape(NEXP, FC, 128, DC, 128).transpose(0, 1, 4, 3, 2)),
        "wsd_t": asc(np.asarray(L0_ffn_Wsd, np.float32).T),
        "wd_t": asc(np.asarray(L0_ffn_Wd, np.float32).transpose(0, 2, 1)),
    }
    in_maps = []
    for c in range(8):
        b, h = c // 2, c % 2
        m = dict(common)
        m["x_mine"] = asc(x[b, h * T:(h + 1) * T])
        m["x_part"] = asc(x[b, (1 - h) * T: (2 - h) * T])
        in_maps.append(m)
    return in_maps


def kernel(**inputs):
    in_maps = prep_inputs(**inputs)
    nc = build_program(debug=False)
    res = run_bass_kernel_spmd(nc, in_maps, core_ids=list(range(8)))
    shards = [res.results[c]["out"] for c in range(8)]
    full = np.stack(shards, axis=0).reshape(4, 4096, D).astype(np.float32)
    return full
